# version 01
# baseline (speedup 1.0000x reference)
import math
from contextlib import ExitStack

import numpy as np
import ml_dtypes

import concourse.bass as bass
import concourse.mybir as mybir
from concourse.bass_utils import run_bass_kernel_spmd

F32 = mybir.dt.float32
BF16 = mybir.dt.bfloat16
I32 = mybir.dt.int32
AF = mybir.ActivationFunctionType
ALU = mybir.AluOpType
AXX = mybir.AxisListType.X

BF = ml_dtypes.bfloat16

NCORES = 8
T = 4096
D = 1024
L = 8192
N1 = 256
N2 = 32
K1 = 129
K1A, K1B = 65, 64
STAGES = 3
EPS = 1e-5
QMAX = 127.0
DI = 2816

ENGINES = ("sync", "vector", "scalar", "tensor", "gpsimd")
DEBUG_L1 = False



_CONSTS = None


def make_consts():
    global _CONSTS
    if _CONSTS is not None:
        return _CONSTS
    c = {}

    pos_t = np.arange(T, dtype=np.float64)[:, None]
    i = np.arange(D, dtype=np.float64)[None, :]
    angle = pos_t / np.power(10000.0, (2 * (i // 2)) / D)
    pos = np.where(i % 2 == 0, np.sin(angle), np.cos(angle))
    c["pos_fm"] = np.ascontiguousarray(pos.T).astype(BF)

    n1 = np.arange(128, dtype=np.float64)[:, None]
    k1 = np.arange(K1, dtype=np.float64)[None, :]
    th = 2.0 * np.pi * n1 * k1 / N1
    c["D1re"] = np.cos(th).astype(BF)
    c["D1im"] = (-np.sin(th)).astype(BF)

    n2 = np.arange(N2, dtype=np.float64)
    k2 = np.arange(N2, dtype=np.float64)
    sre = np.zeros((K1, 128, 128), np.float64)
    sim = np.zeros((K1, 128, 128), np.float64)
    for kk in range(K1):
        ph = (2.0 * np.pi * n2 * kk / L)[:, None] \
            + 2.0 * np.pi * n2[:, None] * k2[None, :] / N2
        br, bi = np.cos(ph), -np.sin(ph)
        for cb in range(4):
            sl = slice(32 * cb, 32 * cb + 32)
            sre[kk][sl, sl] = br
            sim[kk][sl, sl] = bi
    c["BDre"] = sre.astype(BF)
    c["BDim"] = sim.astype(BF)
    c["BDnegim"] = (-sim).astype(BF)
    c["BDcat"] = np.concatenate([sre, sim, -sim], axis=2).astype(BF)

    ph = 2.0 * np.pi * k2[:, None] * n2[None, :] / N2
    iwre = np.zeros((128, 128), np.float64)
    iwim = np.zeros((128, 128), np.float64)
    for cb in range(4):
        sl = slice(32 * cb, 32 * cb + 32)
        iwre[sl, sl] = np.cos(ph)
        iwim[sl, sl] = np.sin(ph)
    c["iWre"] = iwre.astype(BF)
    c["iWim"] = iwim.astype(BF)
    c["iWnegim"] = (-iwim).astype(BF)

    g = np.full(K1, 2.0)
    g[0] = 1.0
    g[128] = 1.0
    kk = np.arange(K1, dtype=np.float64)[:, None]
    tt = np.arange(T, dtype=np.float64)[None, :]
    ph = 2.0 * np.pi * kk * tt / L
    ere = (g[:, None] / L) * np.cos(ph)
    eim = (g[:, None] / L) * np.sin(ph)
    c["EreA"] = ere[:K1A].astype(BF)
    c["EnimA"] = (-eim[:K1A]).astype(BF)
    c["EreB"] = ere[K1A:].astype(BF)
    c["EnimB"] = (-eim[K1A:]).astype(BF)

    c["ident"] = np.eye(128).astype(BF)
    c["identf"] = np.eye(128).astype(np.float32)
    c["ones128"] = np.ones((1, 128)).astype(BF)
    c["ones512"] = np.ones((1, 512)).astype(BF)
    repa = np.zeros((32, 512), np.float32)
    for ck in range(4):
        for d8 in range(8):
            for g in range(16):
                repa[ck * 8 + d8, ck * 128 + d8 * 16 + g] = 1.0
    c["REPA"] = repa
    perio = np.zeros((16, 128), np.float32)
    for d8 in range(8):
        for g in range(16):
            perio[g, d8 * 16 + g] = 1.0
    c["PERIO"] = perio

    _CONSTS = c
    return c



class Plan:

    CHAINED = ("vector", "scalar", "gpsimd")

    def __init__(self, nc, stack):
        self.nc = nc
        self.stack = stack
        self.scopes = [stack]
        self.th = {e: [] for e in ENGINES}
        self._sems = {}

    def sb(self, name, shape, dt):
        return self.scopes[-1].enter_context(self.nc.sbuf_tensor(name, shape, dt))

    def ps(self, name, shape, dt=F32):
        return self.scopes[-1].enter_context(self.nc.psum_tensor(name, shape, dt))

    def full_barrier(self):
        bh, bv = self.inc(self.sem("barrier"), 1)
        self.emit("tensor", lambda e, bh=bh: e.nop().then_inc(bh, 1))
        targets = [("barrier", bv)]
        for nm, s in list(self._sems.items()):
            if nm != "barrier" and s[1] > 0:
                targets.append((nm, s[1]))
        for eng in ENGINES:
            for nm, v in targets:
                self.wait(eng, nm, v)

    class _Scope:
        def __init__(self, p):
            self.p = p

        def __enter__(self):
            es = ExitStack()
            self.p.scopes.append(es)
            return es

        def __exit__(self, *a):
            self.p.full_barrier()
            es = self.p.scopes.pop()
            es.close()
            return False

    def scope(self):
        return Plan._Scope(self)

    def sem(self, name):
        if name not in self._sems:
            h = self.stack.enter_context(self.nc.semaphore(name))
            self._sems[name] = [h, 0]
        return name

    def inc(self, name, k):
        s = self._sems[self.sem(name)]
        s[1] += k
        return s[0], s[1]

    def val(self, name):
        return self._sems[self.sem(name)][1]

    def handle(self, name):
        return self._sems[self.sem(name)][0]

    def emit(self, eng, fn):
        self.th[eng].append(fn)

    def dma(self, eng, out, in_, sem=None, **kw):
        nm = sem if sem is not None else ("dma_" + eng)
        h, v = self.inc(nm, 16)
        self.emit(eng, lambda e, out=out, in_=in_, h=h, kw=kw:
                  e.dma_start(out=out, in_=in_, **kw).then_inc(h, 16))
        return v

    def fence(self):
        self.wait("sync", "dma_sync", self.val(self.sem("dma_sync")))

    def wait(self, eng, sem_name, value=None):
        v = self.val(sem_name) if value is None else value
        h = self.handle(sem_name)
        self.emit(eng, lambda e, h=h, v=v: e.wait_ge(h, v))

    def op(self, eng, fn, extra_sem=None, extra_k=1):
        if eng in self.CHAINED:
            assert extra_sem is None
            nm = "ch_" + eng
            cur = self.val(self.sem(nm))
            h, v = self.inc(nm, 1)

            def thunk(e, fn=fn, h=h, cur=cur):
                if cur:
                    e.wait_ge(h, cur)
                fn(e).then_inc(h, 1)
            self.emit(eng, thunk)
            return v
        else:
            if extra_sem is None:
                self.emit(eng, fn)
                return None
            h, v = self.inc(extra_sem, extra_k)
            self.emit(eng, lambda e, fn=fn, h=h, k=extra_k: fn(e).then_inc(h, k))
            return v

    def chv(self, eng):
        return self.val(self.sem("ch_" + eng))

    def run_block(self):
        nc = self.nc
        with nc.Block() as block:
            @block.sync
            def _(e):
                for t in self.th["sync"]:
                    t(e)

            @block.vector
            def _(e):
                for t in self.th["vector"]:
                    t(e)

            @block.scalar
            def _(e):
                for t in self.th["scalar"]:
                    t(e)

            @block.tensor
            def _(e):
                for t in self.th["tensor"]:
                    t(e)

            @block.gpsimd
            def _(e):
                for t in self.th["gpsimd"]:
                    t(e)


def mk_nc():
    return bass.Bass("TRN2", target_bir_lowering=False, debug=False)


def run_launch(build_fn, in_maps, trace=False):
    nc, out_names = build_fn()
    res = run_bass_kernel_spmd(nc, in_maps, core_ids=list(range(len(in_maps))),
                               trace=trace)
    return res



def emit_ln(p, xs_aps, xn_f32_ap, xsum_ap, pmax_dram_ap, tag, xs_dts=None):
    nin = len(xs_aps)
    dts = xs_dts if xs_dts is not None else [F32] * nin
    xt = p.sb(tag + "xt", [128, D], F32)
    xe = {}
    for dt in set(dts[1:]):
        xe[dt] = p.sb(tag + "xe%d" % len(xe), [128, D], dt)
    sq = p.sb(tag + "sq", [128, D], F32)
    scol = p.sb(tag + "scol", [128, 1], F32)
    mcol = p.sb(tag + "mcol", [128, 1], F32)
    msq = p.sb(tag + "msq", [128, 1], F32)
    vcol = p.sb(tag + "vcol", [128, 1], F32)
    rstd = p.sb(tag + "rstd", [128, 1], F32)
    xnf = p.sb(tag + "xnf", [128, D], F32)
    maxacc = p.sb(tag + "maxacc", [128, 1], F32)
    maxtmp = p.sb(tag + "maxtmp", [128, 1], F32)

    v_done = 0
    for it in range(T // 128):
        t0 = it * 128
        so_prev = p.val(p.sem(tag + "st"))
        if it > 0:
            p.wait("sync", "ch_vector", v_done)
        ldv = p.dma("sync", xt[:, :], xs_aps[0][t0:t0 + 128, :])
        p.wait("vector", "dma_sync", ldv)
        av = 0
        if nin > 1:
            for j in range(1, nin):
                xej = xe[dts[j]]
                ldv = p.dma("sync", xej[:, :], xs_aps[j][t0:t0 + 128, :])
                p.wait("vector", "dma_sync", ldv)
                av = p.op("vector", lambda e, xej=xej: e.tensor_add(
                    xt[:, :], xt[:, :], xej[:, :]))
                if j < nin - 1:
                    p.wait("sync", "ch_vector", av)
            if xsum_ap is not None:
                p.wait("sync", "ch_vector", av)
                p.dma("sync", xsum_ap[t0:t0 + 128, :], xt[:, :], sem=tag + "st")

        p.op("vector", lambda e: e.reduce_sum(scol[:, :], xt[:, :], AXX))
        mv = p.op("vector", lambda e: e.tensor_scalar_mul(mcol[:, :], scol[:, :], 1.0 / D))
        p.wait("scalar", "dma_sync", ldv)
        if av:
            p.wait("scalar", "ch_vector", av)
        if it > 0:
            p.wait("scalar", "ch_vector", v_done)
        sv = p.op("scalar", lambda e: e.activation(sq[:, :], xt[:, :], AF.Square))
        p.wait("vector", "ch_scalar", sv)
        p.op("vector", lambda e: e.reduce_sum(vcol[:, :], sq[:, :], AXX))
        p.op("vector", lambda e: e.tensor_mul(msq[:, :], mcol[:, :], mcol[:, :]))
        p.op("vector", lambda e: e.tensor_scalar_mul(vcol[:, :], vcol[:, :], 1.0 / D))
        p.op("vector", lambda e: e.tensor_sub(vcol[:, :], vcol[:, :], msq[:, :]))
        ev = p.op("vector", lambda e: e.tensor_scalar_add(vcol[:, :], vcol[:, :], float(EPS)))
        p.wait("scalar", "ch_vector", ev)
        qv = p.op("scalar", lambda e: e.activation(rstd[:, :], vcol[:, :], AF.Sqrt))
        p.wait("vector", "ch_scalar", qv)
        p.op("vector", lambda e: e.reciprocal(rstd[:, :], rstd[:, :]))
        p.wait("vector", tag + "st", so_prev)
        nv = p.op("vector", lambda e: e.tensor_scalar(
            xnf[:, :], xt[:, :], mcol[:, 0:1], rstd[:, 0:1],
            ALU.subtract, ALU.mult))
        if pmax_dram_ap is not None:
            p.op("vector", lambda e: e.reduce_max(maxtmp[:, :], xnf[:, :], AXX,
                                                  apply_absolute_value=True))
            if it == 0:
                nv = p.op("vector", lambda e: e.tensor_copy(maxacc[:, :], maxtmp[:, :]))
            else:
                nv = p.op("vector", lambda e: e.tensor_max(maxacc[:, :], maxacc[:, :],
                                                           maxtmp[:, :]))
        v_done = nv
        p.wait("sync", "ch_vector", nv)
        p.dma("sync", xn_f32_ap[t0:t0 + 128, :], xnf[:, :], sem=tag + "st")

    if pmax_dram_ap is not None:
        p.dma("sync", pmax_dram_ap, maxacc[:, :], sem=tag + "st")
    p.wait("sync", "dma_sync", p.val("dma_sync"))
    p.wait("sync", tag + "st", p.val(tag + "st"))


def emit_transpose(p, src_ap, dst_ap, R, C, ident_sb, tag, src_dt=F32,
                   dst_dt=BF16):
    tin = p.sb(tag + "tin", [128, C], src_dt)
    tb = p.sb(tag + "tb", [128, C], dst_dt) if src_dt != dst_dt else tin
    tout = p.sb(tag + "tout", [128, 128], dst_dt)
    pst = p.ps(tag + "pst", [128, 128], dst_dt)
    tsem = p.sem(tag + "t")

    ev = 0
    cast_v = 0
    for it in range(R // 128):
        r0 = it * 128
        if it > 0:
            p.wait("sync", tag + "t", p.val(tag + "t"))
        ldv = p.dma("sync", tin[:, :], src_ap[r0:r0 + 128, :])
        if src_dt != dst_dt:
            p.wait("scalar", "dma_sync", ldv)
            if it > 0:
                p.wait("scalar", tag + "t", p.val(tag + "t"))
            cast_v = p.op("scalar", lambda e: e.copy(tb[:, :], tin[:, :]))
            p.wait("tensor", "ch_scalar", cast_v)
        else:
            p.wait("tensor", "dma_sync", ldv)
        for ck in range(C // 128):
            if ev:
                p.wait("tensor", "ch_vector", ev)
            tv = p.op("tensor", lambda e, ck=ck: e.transpose(
                pst[:, :], tb[:, ck * 128:(ck + 1) * 128], ident_sb[:, :]),
                extra_sem=tag + "t")
            p.wait("vector", tag + "t", tv)
            if ev:
                p.wait("vector", tag + "st", so_prev)
            ev = p.op("vector", lambda e: e.tensor_copy(tout[:, :], pst[:, :]))
            p.wait("sync", "ch_vector", ev)
            so_prev = p.dma("sync", dst_ap[ck * 128:ck * 128 + 128, r0:r0 + 128],
                            tout[:, :], sem=tag + "st")
    p.wait("sync", "dma_sync", p.val("dma_sync"))
    p.wait("sync", tag + "st", p.val(tag + "st"))



def build_ln0():
    nc = mk_nc()
    X = nc.dram_tensor("X", [T, D], F32, kind="ExternalInput").ap()
    xn = nc.dram_tensor("xn_f32", [T, D], F32, kind="ExternalOutput").ap()
    pmax = nc.dram_tensor("pmax", [128, 1], F32, kind="ExternalOutput").ap()
    with ExitStack() as st:
        p = Plan(nc, st)
        with p.scope():
            emit_ln(p, [X], xn, None, pmax, "ln0")
        p.run_block()
    return nc, ["xn_f32", "pmax"]


def emit_scalar_combine(p, pmax_all_ap, clip_ap, out_scale_ap, out_recip_sb, tag,
                        extra_mul=1.0):
    nc = p.nc
    pm = p.sb(tag + "pm", [8, 128], F32)
    red = p.sb(tag + "red", [8, 1], F32)
    row = p.sb(tag + "row", [1, 8], F32)
    g1 = p.sb(tag + "g1", [1, 1], F32)
    cl = p.sb(tag + "cl", [1, 1], F32)
    sc = p.sb(tag + "sc", [1, 1], F32)
    rc = p.sb(tag + "rc", [1, 1], F32)
    scr = nc.dram_tensor(tag + "scr", [8, 1], F32).ap()
    scr2 = nc.dram_tensor(tag + "scr2", [1, 1], F32).ap()

    v = p.dma("sync", pm[:, :], pmax_all_ap)
    v2 = p.dma("sync", cl[:, :], clip_ap)
    p.wait("vector", "dma_sync", v2)
    rv = p.op("vector", lambda e: e.reduce_max(red[:, :], pm[:, :], AXX))
    p.wait("sync", "ch_vector", rv)
    sv = p.dma("sync", scr, red[:, :])
    p.wait("sync", "dma_sync", sv)
    lv = p.dma("sync", row[:, :], scr.rearrange("a 1 -> 1 a"))
    p.wait("vector", "dma_sync", lv)
    p.op("vector", lambda e: e.reduce_max(g1[:, :], row[:, :], AXX))
    p.wait("scalar", "dma_sync", v2)
    gv = p.op("scalar", lambda e: e.activation(cl[:, :], cl[:, :], AF.Sigmoid))
    p.wait("vector", "ch_scalar", gv)
    p.op("vector", lambda e: e.tensor_mul(sc[:, :], g1[:, :], cl[:, :]))
    scv = p.op("vector", lambda e: e.tensor_scalar_mul(sc[:, :], sc[:, :],
                                                       float(extra_mul) / QMAX))
    p.wait("sync", "ch_vector", scv)
    p.dma("sync", out_scale_ap, sc[:, :])
    rcv = p.op("vector", lambda e: e.reciprocal(rc[:, :], sc[:, :]))
    p.wait("sync", "ch_vector", rcv)
    bv = p.dma("sync", scr2, rc[:, :])
    p.wait("sync", "dma_sync", bv)
    bb = p.dma("sync", out_recip_sb[:, :], scr2.broadcast_to([128, 1]))
    return bb


def emit_quant(p, src_f32_ap, dst_bf16_ap, R, C, recip_sb, recip_ready, tag,
               do_clip=True):
    xt = p.sb(tag + "xt", [128, C], F32)
    qf = p.sb(tag + "qf", [128, C], F32)
    qi = p.sb(tag + "qi", [128, C], I32)
    qb = p.sb(tag + "qb", [128, C], BF16)
    p.wait("vector", "dma_sync", recip_ready)
    last = 0
    for it in range(R // 128):
        r0 = it * 128
        if it > 0:
            p.wait("sync", "ch_vector", last)
        ldv = p.dma("sync", xt[:, :], src_f32_ap[r0:r0 + 128, :])
        p.wait("vector", "dma_sync", ldv)
        if it > 0:
            p.wait("vector", tag + "st", so_prev)
        p.op("vector", lambda e: e.tensor_scalar_mul(qf[:, :], xt[:, :], recip_sb[:, 0:1]))
        p.op("vector", lambda e: e.tensor_copy(qi[:, :], qf[:, :]))
        if do_clip:
            p.op("vector", lambda e: e.tensor_scalar(qi[:, :], qi[:, :], 127, -128,
                                                     ALU.min, ALU.max))
        last = p.op("vector", lambda e: e.tensor_copy(qb[:, :], qi[:, :]))
        p.wait("sync", "ch_vector", last)
        so_prev = p.dma("sync", dst_bf16_ap[r0:r0 + 128, :], qb[:, :], sem=tag + "st")
    p.wait("sync", "dma_sync", p.val("dma_sync"))
    p.wait("sync", tag + "st", p.val(tag + "st"))


def emit_cast(p, src_f32_ap, dst_bf16_ap, R, C, tag):
    xt = p.sb(tag + "xt", [128, C], F32)
    xb = p.sb(tag + "xb", [128, C], BF16)
    last = 0
    for it in range(R // 128):
        r0 = it * 128
        if it > 0:
            p.wait("sync", "ch_scalar", last)
        ldv = p.dma("sync", xt[:, :], src_f32_ap[r0:r0 + 128, :])
        p.wait("scalar", "dma_sync", ldv)
        if it > 0:
            p.wait("scalar", tag + "st", so_prev)
        last = p.op("scalar", lambda e: e.copy(xb[:, :], xt[:, :]))
        p.wait("sync", "ch_scalar", last)
        so_prev = p.dma("sync", dst_bf16_ap[r0:r0 + 128, :], xb[:, :], sem=tag + "st")
    p.wait("sync", "dma_sync", p.val("dma_sync"))
    p.wait("sync", tag + "st", p.val(tag + "st"))


def build_qt():
    nc = mk_nc()
    xn = nc.dram_tensor("xn_f32", [T, D], F32, kind="ExternalInput").ap()
    pmax_all = nc.dram_tensor("pmax_all", [8, 128], F32, kind="ExternalInput").ap()
    clip_x = nc.dram_tensor("clip_x", [1, 1], F32, kind="ExternalInput").ap()
    identc = nc.dram_tensor("identc", [128, 128], BF16, kind="ExternalInput").ap()
    xn_fm = nc.dram_tensor("xn_fm", [D, T], BF16, kind="ExternalOutput").ap()
    xq_fm = nc.dram_tensor("xq_fm", [D, T], BF16, kind="ExternalOutput").ap()
    scale_x = nc.dram_tensor("scale_x", [1, 1], F32, kind="ExternalOutput").ap()
    xq_tm = nc.dram_tensor("xq_tm", [T, D], BF16).ap()
    xn_tm = nc.dram_tensor("xn_tm", [T, D], BF16).ap()

    with ExitStack() as st:
        p = Plan(nc, st)
        ident = p.sb("ident", [128, 128], BF16)
        iv = p.dma("sync", ident[:, :], identc)
        recip = p.sb("recip", [128, 1], F32)
        with p.scope():
            rv = emit_scalar_combine(p, pmax_all, clip_x, scale_x, recip, "sc")
        with p.scope():
            emit_quant(p, xn, xq_tm, T, D, recip, p.val("dma_sync"), "q")
        with p.scope():
            emit_cast(p, xn, xn_tm, T, D, "cb")
        p.wait("tensor", "dma_sync", iv)
        with p.scope():
            emit_transpose(p, xq_tm, xq_fm, T, D, ident, "t1", src_dt=BF16)
        with p.scope():
            emit_transpose(p, xn_tm, xn_fm, T, D, ident, "t2", src_dt=BF16)
        p.run_block()
    return nc, ["xn_fm", "xq_fm", "scale_x"]


def emit_wquant(p, w_ap, wq_tm_ap, wscale_ap, R, C, tag):
    wt = p.sb(tag + "wt", [128, C], F32)
    wmax = p.sb(tag + "wmax", [128, 1], F32)
    wrec = p.sb(tag + "wrec", [128, 1], F32)
    qf = p.sb(tag + "qf", [128, C], F32)
    qi = p.sb(tag + "qi", [128, C], I32)
    qb = p.sb(tag + "qb", [128, C], BF16)
    last = 0
    for it in range(R // 128):
        r0 = it * 128
        if it > 0:
            p.wait("sync", "ch_vector", last)
        ldv = p.dma("sync", wt[:, :], w_ap[r0:r0 + 128, :])
        p.wait("vector", "dma_sync", ldv)
        if it > 0:
            p.wait("vector", tag + "st", so_prev)
        p.op("vector", lambda e: e.reduce_max(wmax[:, :], wt[:, :], AXX,
                                              apply_absolute_value=True))
        sv = p.op("vector", lambda e: e.tensor_scalar_mul(wmax[:, :], wmax[:, :],
                                                          1.0 / QMAX))
        p.wait("sync", "ch_vector", sv)
        p.dma("sync", wscale_ap[r0:r0 + 128, :], wmax[:, :], sem=tag + "st")
        p.op("vector", lambda e: e.reciprocal(wrec[:, :], wmax[:, :]))
        p.op("vector", lambda e: e.tensor_scalar_mul(qf[:, :], wt[:, :], wrec[:, 0:1]))
        p.op("vector", lambda e: e.tensor_copy(qi[:, :], qf[:, :]))
        last = p.op("vector", lambda e: e.tensor_copy(qb[:, :], qi[:, :]))
        p.wait("sync", "ch_vector", last)
        so_prev = p.dma("sync", wq_tm_ap[r0:r0 + 128, :], qb[:, :], sem=tag + "st")
    p.wait("sync", "dma_sync", p.val("dma_sync"))
    p.wait("sync", tag + "st", p.val(tag + "st"))


def build_ffn0a():
    nc = mk_nc()
    xq_fm = nc.dram_tensor("xq_fm", [D, T], BF16, kind="ExternalInput").ap()
    scale_x = nc.dram_tensor("scale_x", [1, 1], F32, kind="ExternalInput").ap()
    w_gate = nc.dram_tensor("w_gate", [DI, D], F32, kind="ExternalInput").ap()
    w_up = nc.dram_tensor("w_up", [DI, D], F32, kind="ExternalInput").ap()
    identc = nc.dram_tensor("identc", [128, 128], BF16, kind="ExternalInput").ap()
    h_out = nc.dram_tensor("h_f32", [DI, T], F32, kind="ExternalOutput").ap()
    pmax_h = nc.dram_tensor("pmax_h", [128, 1], F32, kind="ExternalOutput").ap()

    wgq_tm = nc.dram_tensor("wgq_tm", [DI, D], BF16).ap()
    wuq_tm = nc.dram_tensor("wuq_tm", [DI, D], BF16).ap()
    wgq_fm = nc.dram_tensor("wgq_fm", [D, DI], BF16).ap()
    wuq_fm = nc.dram_tensor("wuq_fm", [D, DI], BF16).ap()
    wgs = nc.dram_tensor("wgs", [DI, 1], F32).ap()
    wus = nc.dram_tensor("wus", [DI, 1], F32).ap()

    TC = 512
    with ExitStack() as st:
        p = Plan(nc, st)
        ident = p.sb("ident", [128, 128], BF16)
        iv = p.dma("sync", ident[:, :], identc)
        with p.scope():
            emit_wquant(p, w_gate, wgq_tm, wgs, DI, D, "wg")
        with p.scope():
            emit_wquant(p, w_up, wuq_tm, wus, DI, D, "wu")
        p.wait("tensor", "dma_sync", iv)
        with p.scope():
            emit_transpose(p, wgq_tm, wgq_fm, DI, D, ident, "tg", src_dt=BF16)
        with p.scope():
            emit_transpose(p, wuq_tm, wuq_fm, DI, D, ident, "tu", src_dt=BF16)

        scx = p.sb("scx", [128, 1], F32)
        sv = p.dma("sync", scx[:, :], scale_x.broadcast_to([128, 1]))

        wg_sb = p.sb("wg_sb", [128, 8 * DI], BF16)
        wu_sb = p.sb("wu_sb", [128, 8 * DI], BF16)
        xq_sb = p.sb("xq_sb", [128, 8 * TC], BF16)
        wv = 0
        for k in range(8):
            wv = p.dma("sync", wg_sb[:, k * DI:(k + 1) * DI], wgq_fm[k * 128:(k + 1) * 128, :])
            wv = p.dma("sync", wu_sb[:, k * DI:(k + 1) * DI], wuq_fm[k * 128:(k + 1) * 128, :])
        gsc = p.sb("gsc", [128, 22], F32)
        usc = p.sb("usc", [128, 22], F32)
        p.fence()
        g1 = p.dma("sync", gsc[:, :], wgs.rearrange("(a p) 1 -> p a", p=128), allow_slow_non_contiguous=True)
        g2 = p.dma("sync", usc[:, :], wus.rearrange("(a p) 1 -> p a", p=128), allow_slow_non_contiguous=True)
        p.fence()
        p.wait("vector", "dma_sync", max(sv, g1, g2))
        p.op("vector", lambda e: e.tensor_scalar_mul(gsc[:, :], gsc[:, :], scx[:, 0:1]))
        cv = p.op("vector", lambda e: e.tensor_scalar_mul(usc[:, :], usc[:, :], scx[:, 0:1]))
        p.wait("scalar", "ch_vector", cv)

        pg = p.ps("pg", [128, TC])
        pu = p.ps("pu", [128, TC])
        sg = p.sb("sg", [128, TC], F32)
        su = p.sb("su", [128, TC], F32)
        ht = p.sb("ht", [128, TC], F32)
        maxacc = p.sb("maxacc", [128, 1], F32)
        maxtmp = p.sb("maxtmp", [128, 1], F32)
        zv = p.op("vector", lambda e: e.memset(maxacc[:, :], 0.0))

        first = True
        ev_g = ev_u = hv = 0
        st_prev = 0
        p.sem("mmend")
        for tc in range(T // TC):
            t0 = tc * TC
            if tc > 0:
                p.wait("sync", "mmend", 22 * tc)
            for k in range(8):
                xv = p.dma("sync", xq_sb[:, k * TC:(k + 1) * TC],
                           xq_fm[k * 128:(k + 1) * 128, t0:t0 + TC])
            p.wait("tensor", "dma_sync", xv)
            for ic in range(22):
                i0 = ic * 128
                if not first:
                    p.wait("tensor", "ch_scalar", ev_u)
                for k in range(8):
                    p.op("tensor", lambda e, k=k, i0=i0: e.matmul(
                        pg[:, :], wg_sb[:, k * DI + i0:k * DI + i0 + 128],
                        xq_sb[:, k * TC:(k + 1) * TC],
                        start=(k == 0), stop=(k == 7)))
                for k in range(8):
                    p.op("tensor", lambda e, k=k, i0=i0: e.matmul(
                        pu[:, :], wu_sb[:, k * DI + i0:k * DI + i0 + 128],
                        xq_sb[:, k * TC:(k + 1) * TC],
                        start=(k == 0), stop=(k == 7)), extra_sem="mmend" if k == 7 else None)
                mv = p.val("mmend")
                p.wait("scalar", "mmend", mv)
                if not first:
                    p.wait("scalar", "ch_vector", hv)
                p.op("scalar", lambda e, ic=ic: e.activation(
                    sg[:, :], pg[:, :], AF.Silu, scale=gsc[:, ic:ic + 1]))
                ev_u = p.op("scalar", lambda e, ic=ic: e.activation(
                    su[:, :], pu[:, :], AF.Copy, scale=usc[:, ic:ic + 1]))
                p.wait("vector", "ch_scalar", ev_u)
                if not first:
                    p.wait("vector", "fa_st", st_prev)
                hv = p.op("vector", lambda e: e.tensor_mul(ht[:, :], sg[:, :], su[:, :]))
                p.op("vector", lambda e: e.reduce_max(maxtmp[:, :], ht[:, :], AXX,
                                                      apply_absolute_value=True))
                hv = p.op("vector", lambda e: e.tensor_max(maxacc[:, :], maxacc[:, :],
                                                           maxtmp[:, :]))
                p.wait("sync", "ch_vector", hv)
                st_prev = p.dma("sync", h_out[i0:i0 + 128, t0:t0 + TC], ht[:, :],
                                sem="fa_st")
                first = False
        lv = p.op("vector", lambda e: e.tensor_copy(maxtmp[:, :], maxacc[:, :]))
        p.wait("sync", "ch_vector", lv)
        p.dma("sync", pmax_h, maxacc[:, :], sem="fa_st")
        p.wait("sync", "dma_sync", p.val("dma_sync"))
        p.wait("sync", "fa_st", p.val("fa_st"))
        p.run_block()
    return nc, ["h_f32", "pmax_h"]


def build_ffn0b():
    nc = mk_nc()
    h_f32 = nc.dram_tensor("h_f32", [DI, T], F32, kind="ExternalInput").ap()
    pmax_all = nc.dram_tensor("pmax_h_all", [8, 128], F32, kind="ExternalInput").ap()
    clip_down = nc.dram_tensor("clip_down", [1, 1], F32, kind="ExternalInput").ap()
    w_down = nc.dram_tensor("w_down", [D, DI], F32, kind="ExternalInput").ap()
    identc = nc.dram_tensor("identc", [128, 128], BF16, kind="ExternalInput").ap()
    ffn0 = nc.dram_tensor("ffn0", [T, D], F32, kind="ExternalOutput").ap()

    hq_tm = nc.dram_tensor("hq_tm", [DI, T], BF16).ap()
    wdq_tm = nc.dram_tensor("wdq_tm", [D, DI], BF16).ap()
    wdq_fm = nc.dram_tensor("wdq_fm", [DI, D], BF16).ap()
    wds = nc.dram_tensor("wds", [D, 1], F32).ap()
    scale_h_d = nc.dram_tensor("scale_h_d", [1, 1], F32).ap()
    rowsc_d = nc.dram_tensor("rowsc_d", [1, D], F32).ap()

    with ExitStack() as st:
        p = Plan(nc, st)
        ident = p.sb("ident", [128, 128], BF16)
        iv = p.dma("sync", ident[:, :], identc)
        recip = p.sb("recip", [128, 1], F32)
        with p.scope():
            rv = emit_scalar_combine(p, pmax_all, clip_down, scale_h_d, recip, "sc")
        with p.scope():
            emit_quant(p, h_f32, hq_tm, DI, T, recip, p.val("dma_sync"), "q")
        with p.scope():
            emit_wquant(p, w_down, wdq_tm, wds, D, DI, "wd")
        p.wait("tensor", "dma_sync", iv)
        with p.scope():
            emit_transpose(p, wdq_tm, wdq_fm, D, DI, ident, "td", src_dt=BF16)

        shc = p.sb("shc", [128, 1], F32)
        s1 = p.dma("sync", shc[:, :], scale_h_d.broadcast_to([128, 1]))
        wsc = p.sb("wsc", [128, 8], F32)
        s2 = p.dma("sync", wsc[:, :], wds.rearrange("(a p) 1 -> p a", p=128), allow_slow_non_contiguous=True)
        p.wait("vector", "dma_sync", max(s1, s2))
        sv = p.op("vector", lambda e: e.tensor_scalar_mul(wsc[:, :], wsc[:, :], shc[:, 0:1]))
        p.wait("sync", "ch_vector", sv)
        s3 = p.dma("sync", rowsc_d.rearrange("o (a p) -> (o p) a", p=128), wsc[:, :], allow_slow_non_contiguous=True)
        p.wait("sync", "dma_sync", s3)
        rowsc = p.sb("rowsc", [128, D], F32)
        s4 = p.dma("sync", rowsc[:, :], rowsc_d.broadcast_to([128, D]))
        p.fence()

        hq_sb = p.sb("hq_sb", [128, 22 * T], BF16)
        for k in range(22):
            hv = p.dma("sync", hq_sb[:, k * T:(k + 1) * T],
                       hq_tm[k * 128:(k + 1) * 128, :])
        p.fence()
        p.wait("tensor", "dma_sync", p.val("dma_sync"))
        wd_sl = [p.sb("wd_sl%d" % b, [128, 512], BF16) for b in range(2)]
        pd = [p.ps("pd%d" % b, [128, 512]) for b in range(4)]
        ot = p.sb("ot", [128, 512], F32)

        NG = 4
        DEBUG_SERIAL = False
        ev = 0
        stv = 0
        wready = {}
        p.wait("vector", "dma_sync", s4)
        p.sem("wfree")
        p.sem("dwd0")
        p.sem("dwd1")
        for grp in range(32 // NG):
            for hh in range(2):
                for k in range(22):
                    wb = k % 2
                    tgt = p.val("wfree") - 1
                    if tgt > 0:
                        p.wait("sync", "wfree", tgt)
                    wv = p.dma("sync", wd_sl[wb][:, :],
                               wdq_fm[k * 128:(k + 1) * 128, hh * 512:hh * 512 + 512],
                               sem="dwd%d" % wb)
                    p.wait("tensor", "dwd%d" % wb, wv)
                    if k == 0 and ev:
                        p.wait("tensor", "ch_vector", ev)
                    for j in range(NG):
                        n2 = grp * NG + j
                        p.op("tensor", lambda e, k=k, n2=n2, j=j, wb=wb: e.matmul(
                            pd[j][:, :],
                            hq_sb[:, k * T + n2:k * T + n2 + 4065:32],
                            wd_sl[wb][:, :],
                            start=(k == 0), stop=(k == 21)),
                            extra_sem="wfree" if j == NG - 1 else None)
                mv = p.val("wfree")
                p.wait("vector", "wfree", mv)
                for j in range(NG):
                    n2 = grp * NG + j
                    if stv:
                        p.wait("vector", "fb_st", stv)
                    ev = p.op("vector", lambda e, j=j, hh=hh: e.tensor_mul(
                        ot[:, :], pd[j][:, :], rowsc[:, hh * 512:hh * 512 + 512]))
                    p.wait("sync", "ch_vector", ev)
                    stv = p.dma("sync", ffn0[n2:T:32, hh * 512:hh * 512 + 512], ot[:, :],
                                sem="fb_st")
                if DEBUG_SERIAL:
                    p.wait("tensor", "dma_sync", stv)
        p.wait("sync", "dma_sync", p.val("dma_sync"))
        p.wait("sync", "fb_st", p.val("fb_st"))
        p.run_block()
    return nc, ["ffn0"]


def build_l1():
    nc = mk_nc()
    xn1_fm = nc.dram_tensor("xn1_fm", [D, T], BF16, kind="ExternalInput").ap()
    x1 = nc.dram_tensor("x1", [T, D], F32, kind="ExternalInput").ap()
    Wz = {}
    bz = {}
    for nm in ("zf", "hf", "zs", "hs"):
        Wz[nm] = nc.dram_tensor("W_" + nm, [D, 512], F32, kind="ExternalInput").ap()
        bz[nm] = nc.dram_tensor("b_" + nm, [512, 1], F32, kind="ExternalInput").ap()
    W_g = nc.dram_tensor("W_g", [D, D], F32, kind="ExternalInput").ap()
    b_g = nc.dram_tensor("b_g", [D, 1], F32, kind="ExternalInput").ap()
    G1R = nc.dram_tensor("G1_R", [128, 32], F32, kind="ExternalInput").ap()
    G2R = nc.dram_tensor("G2_R", [64, 16], F32, kind="ExternalInput").ap()
    G1I = nc.dram_tensor("G1_I", [128, 32], F32, kind="ExternalInput").ap()
    G2I = nc.dram_tensor("G2_I", [64, 16], F32, kind="ExternalInput").ap()
    b_re = nc.dram_tensor("bias_real", [1, 512], F32, kind="ExternalInput").ap()
    b_im = nc.dram_tensor("bias_imag", [1, 512], F32, kind="ExternalInput").ap()
    onesc = nc.dram_tensor("onesc", [1, 128], BF16, kind="ExternalInput").ap()
    ones512c = nc.dram_tensor("ones512c", [1, 512], BF16, kind="ExternalInput").ap()
    repac = nc.dram_tensor("repac", [32, 512], F32, kind="ExternalInput").ap()
    perioc = nc.dram_tensor("perioc", [16, 128], F32, kind="ExternalInput").ap()
    identc = nc.dram_tensor("identc", [128, 128], BF16, kind="ExternalInput").ap()
    identf = nc.dram_tensor("identf", [128, 128], F32, kind="ExternalInput").ap()
    OUT = nc.dram_tensor("OUT", [T, D], F32, kind="ExternalOutput").ap()
    kind = "ExternalOutput" if DEBUG_L1 else "Internal"
    h_dbg = nc.dram_tensor("h_dbg", [D, T], BF16, kind=kind).ap()
    wr_dbg = nc.dram_tensor("wr_dbg", [512, 512], BF16, kind=kind).ap()
    wi_dbg = nc.dram_tensor("wi_dbg", [512, 512], BF16, kind=kind).ap()
    attn1_fm = nc.dram_tensor("attn1_fm", [D, T], BF16, kind=kind).ap()
    attn1_tm = nc.dram_tensor("attn1_tm", [T, D], BF16, kind=kind).ap()

    with ExitStack() as st:
        p = Plan(nc, st)
        ident = p.sb("ident", [128, 128], BF16)
        iv = p.dma("sync", ident[:, :], identc)
        identF = p.sb("identF", [128, 128], F32)
        p.dma("sync", identF[:, :], identf)
        ones = p.sb("ones", [1, 128], BF16)
        p.dma("sync", ones[:, :], onesc)
        ones512 = p.sb("ones512", [1, 512], BF16)
        p.dma("sync", ones512[:, :], ones512c)
        xn1_sb = p.sb("xn1_sb", [128, 8 * T], BF16)
        for k in range(8):
            xv = p.dma("sync", xn1_sb[:, k * T:(k + 1) * T],
                       xn1_fm[k * 128:(k + 1) * 128, :])
        p.fence()
        p.wait("tensor", "dma_sync", p.val("dma_sync"))

        with p.scope() as mid:
            h_sb = self_h = p.scopes[-1].enter_context(
                nc.sbuf_tensor("h_sb", [128, 8 * T], BF16))

            def _phaseA():
                wsl = p.sb("wslA", [128, 8 * 128], BF16)
                wtmp = p.sb("wtmpA", [128, 128], F32)
                browf = p.sb("browfA", [1, 128], F32)
                brow = p.sb("browA", [1, 128], BF16)
                zrow = p.sb("zrowA", [128, T], F32)
                hrow = p.sb("hrowA", [128, T], F32)
                pp = [p.ps("ppA%d" % b, [128, 512]) for b in range(2)]
                for fc in range(8):
                    fam = "f" if fc < 4 else "s"
                    sub = fc % 4
                    for which in ("z", "h"):
                        wap = Wz[which + fam]
                        for k in range(8):
                            lv = p.dma("sync", wtmp[:, :],
                                       wap[k * 128:(k + 1) * 128,
                                           sub * 128:sub * 128 + 128])
                            p.wait("scalar", "dma_sync", lv)
                            cv = p.op("scalar", lambda e, k=k: e.copy(
                                wsl[:, k * 128:(k + 1) * 128], wtmp[:, :]))
                            p.wait("sync", "ch_scalar", cv)
                        bv = p.dma("sync", browf[:, :],
                                   bz[which + fam].rearrange("a o -> o a")[
                                       :, sub * 128:sub * 128 + 128])
                        p.wait("scalar", "dma_sync", bv)
                        cb = p.op("scalar", lambda e: e.copy(brow[:, :], browf[:, :]))
                        p.wait("tensor", "ch_scalar", cb)
                        for tc2 in range(8):
                            b = tc2 % 2
                            for k in range(8):
                                p.op("tensor", lambda e, k=k, tc2=tc2, b=b: e.matmul(
                                    pp[b][:, :], wsl[:, k * 128:(k + 1) * 128],
                                    xn1_sb[:, k * T + tc2 * 512:k * T + tc2 * 512 + 512],
                                    start=(k == 0), stop=False))
                            p.op("tensor", lambda e, b=b: e.matmul(
                                pp[b][:, :], brow[:, :], ones512[:, :],
                                start=False, stop=True), extra_sem="l1mm")
                            mv = p.val("l1mm")
                            p.wait("scalar", "l1mm", mv)
                            if which == "z":
                                ev = p.op("scalar", lambda e, tc2=tc2, b=b: e.activation(
                                    zrow[:, tc2 * 512:tc2 * 512 + 512], pp[b][:, :],
                                    AF.Sigmoid))
                            else:
                                ev = p.op("scalar", lambda e, tc2=tc2, b=b: e.copy(
                                    hrow[:, tc2 * 512:tc2 * 512 + 512], pp[b][:, :]))
                            p.wait("tensor", "ch_scalar", ev)
                    p.wait("vector", "ch_scalar", ev)
                    p.op("vector", lambda e: e.tensor_mul(hrow[:, :], zrow[:, :],
                                                          hrow[:, :]))
                    p.op("vector", lambda e: e.tensor_scalar(
                        zrow[:, :], zrow[:, :], -1.0, 1.0, ALU.mult, ALU.add))
                    p.op("vector", lambda e: e.tensor_tensor_scan(
                        zrow[:, :], zrow[:, :], hrow[:, :], 0.0, ALU.mult, ALU.add))
                    sv = p.op("vector", lambda e, fc=fc: e.tensor_copy(
                        h_sb[:, fc * T:(fc + 1) * T], zrow[:, :]))
                    p.wait("scalar", "ch_vector", sv)
                    p.wait("tensor", "ch_vector", sv)
                    if DEBUG_L1:
                        p.wait("sync", "ch_vector", sv)
                        p.dma("sync", h_dbg[fc * 128:(fc + 1) * 128, :],
                              h_sb[:, fc * T:(fc + 1) * T])

            with p.scope():
                _phaseA()

            def _phaseB():
                wgl = p.sb("wglB", [128, 8 * 128], BF16)
                wtmp = p.sb("wtmpB", [128, 128], F32)
                browf = p.sb("browfB", [1, 128], F32)
                brow = p.sb("browB", [1, 128], BF16)
                gt = p.sb("gtB", [128, 512], F32)
                at_f = p.sb("at_fB", [128, 512], F32)
                at_b = p.sb("at_bB", [128, 512], BF16)
                pp = [p.ps("ppB%d" % b, [128, 512]) for b in range(2)]
                stv = 0
                gtv = 0
                for dc in range(8):
                    for k in range(8):
                        lv = p.dma("sync", wtmp[:, :],
                                   W_g[k * 128:(k + 1) * 128, dc * 128:dc * 128 + 128])
                        p.wait("scalar", "dma_sync", lv)
                        cv = p.op("scalar", lambda e, k=k: e.copy(
                            wgl[:, k * 128:(k + 1) * 128], wtmp[:, :]))
                        p.wait("sync", "ch_scalar", cv)
                    bv = p.dma("sync", browf[:, :],
                               b_g.rearrange("a o -> o a")[:, dc * 128:dc * 128 + 128])
                    p.wait("scalar", "dma_sync", bv)
                    cb = p.op("scalar", lambda e: e.copy(brow[:, :], browf[:, :]))
                    p.wait("tensor", "ch_scalar", cb)
                    for tc2 in range(8):
                        b = tc2 % 2
                        for k in range(8):
                            p.op("tensor", lambda e, k=k, tc2=tc2, b=b: e.matmul(
                                pp[b][:, :], wgl[:, k * 128:(k + 1) * 128],
                                h_sb[:, k * T + tc2 * 512:k * T + tc2 * 512 + 512],
                                start=(k == 0), stop=False))
                        p.op("tensor", lambda e, b=b: e.matmul(
                            pp[b][:, :], brow[:, :], ones512[:, :],
                            start=False, stop=True), extra_sem="l1mm")
                        mv = p.val("l1mm")
                        p.wait("scalar", "l1mm", mv)
                        if gtv:
                            p.wait("scalar", "ch_vector", gtv)
                        ev = p.op("scalar", lambda e, b=b: e.activation(
                            gt[:, :], pp[b][:, :], AF.Sigmoid))
                        p.wait("tensor", "ch_scalar", ev)
                        p.wait("vector", "ch_scalar", ev)
                        if stv:
                            p.wait("vector", "pb_st", stv)
                        gtv = p.op("vector", lambda e, dc=dc, tc2=tc2: e.tensor_mul(
                            at_f[:, :], gt[:, :],
                            h_sb[:, dc * T + tc2 * 512:dc * T + tc2 * 512 + 512]))
                        av = p.op("vector", lambda e: e.tensor_copy(at_b[:, :],
                                                                    at_f[:, :]))
                        p.wait("sync", "ch_vector", av)
                        stv = p.dma("sync",
                                    attn1_fm[dc * 128:(dc + 1) * 128,
                                             tc2 * 512:tc2 * 512 + 512], at_b[:, :],
                                    sem="pb_st")
                p.wait("sync", "dma_sync", p.val("dma_sync"))
                p.wait("sync", "pb_st", p.val("pb_st"))

            with p.scope():
                _phaseB()

        with p.scope():
            p.wait("tensor", "dma_sync", iv)
            emit_transpose(p, attn1_fm, attn1_tm, D, T, ident, "tat", src_dt=BF16)

        wr_sb = p.sb("wr_sb", [128, 4 * 512], BF16)
        wi_sb = p.sb("wi_sb", [128, 4 * 512], BF16)
        wineg_sb = p.sb("wineg_sb", [128, 4 * 512], BF16)
        bre_b = p.sb("bre_b", [1, 512], BF16)
        bim_b = p.sb("bim_b", [1, 512], BF16)
        def _phaseD():
            g1t = p.sb("g1t", [128, 32], F32)
            g2t = p.sb("g2t", [64, 16], F32)
            g1T = p.sb("g1T", [32, 128], F32)
            g2T = p.sb("g2T", [16, 64], F32)
            repa = p.sb("repa", [32, 512], F32)
            perio = p.sb("perio", [16, 128], F32)
            aee = p.sb("aee", [32, 512], F32)
            bee = p.sb("bee", [16, 512], F32)
            afull = p.sb("afull", [128, 512], F32)
            bfull = [p.sb("bfull%d" % c, [128, 512], F32) for c in range(4)]
            wacc = p.sb("wacc", [128, 512], F32)
            wtmp2 = p.sb("wtmp2", [128, 512], F32)
            brow = p.sb("brow", [1, 512], F32)
            pstf = p.ps("pstf", [128, 128], F32)
            pw = p.ps("pw", [128, 512], F32)
            bv1 = p.dma("sync", brow[:, :], b_re)
            p.wait("scalar", "dma_sync", bv1)
            c1 = p.op("scalar", lambda e: e.copy(bre_b[:, :], brow[:, :]))
            p.wait("sync", "ch_scalar", c1)
            bv2 = p.dma("sync", brow[:, :], b_im)
            p.wait("scalar", "dma_sync", bv2)
            p.op("scalar", lambda e: e.copy(bim_b[:, :], brow[:, :]))
            rv = p.dma("sync", repa[:, :], repac)
            rv = p.dma("sync", perio[:, :], perioc)
            p.fence()
            p.wait("tensor", "dma_sync", p.val("dma_sync"))
            for var, (g1ap, g2ap) in (("i", (G1I, G2I)), ("r", (G1R, G2R))):
                wdst = wr_sb if var == "r" else wi_sb
                lv1 = p.dma("sync", g1t[:, :], g1ap)
                lv2 = p.dma("sync", g2t[:, :], g2ap)
                p.fence()
                p.wait("tensor", "dma_sync", lv2)
                t1 = p.op("tensor", lambda e: e.transpose(
                    pstf[0:32, :], g1t[:, :], identF[:, :]), extra_sem="l1mm")
                p.wait("vector", "l1mm", t1)
                e1 = p.op("vector", lambda e: e.tensor_copy(g1T[:, :], pstf[0:32, :]))
                p.wait("tensor", "ch_vector", e1)
                t2 = p.op("tensor", lambda e: e.transpose(
                    pstf[0:16, 0:64], g2t[:, :], identF[0:64, 0:64]), extra_sem="l1mm")
                p.wait("vector", "l1mm", t2)
                e2 = p.op("vector", lambda e: e.tensor_copy(g2T[:, :], pstf[0:16, 0:64]))
                for c in range(4):
                    bv = p.op("vector", lambda e, c=c: e.tensor_copy(
                        bee[:, :].rearrange("p (b e) -> p b e", e=16),
                        g2T[:, c * 16:(c + 1) * 16].rearrange(
                            "p (o e) -> p o e", o=1).broadcast_to([16, 32, 16])))
                    p.wait("tensor", "ch_vector", bv)
                    m1 = p.op("tensor", lambda e, c=c: e.matmul(
                        pw[:, :], perio[:, :], bee[:, :], start=True, stop=True),
                        extra_sem="l1mm")
                    p.wait("vector", "l1mm", m1)
                    ev = p.op("vector", lambda e, c=c: e.tensor_copy(
                        bfull[c][:, :], pw[:, :]))
                    p.wait("tensor", "ch_vector", ev)
                for ck in range(4):
                    for c in range(4):
                        av = p.op("vector", lambda e, c=c: e.tensor_copy(
                            aee[:, :].rearrange("p (b e) -> p b e", e=16),
                            g1T[:, c:128:4].rearrange(
                                "p (b o) -> p b o", o=1).broadcast_to([32, 32, 16])))
                        p.wait("tensor", "ch_vector", av)
                        m2 = p.op("tensor", lambda e, ck=ck: e.matmul(
                            pw[:, :], repa[:, ck * 128:(ck + 1) * 128], aee[:, :],
                            start=True, stop=True), extra_sem="l1mm")
                        p.wait("vector", "l1mm", m2)
                        ev2 = p.op("vector", lambda e: e.tensor_copy(afull[:, :],
                                                                     pw[:, :]))
                        p.wait("tensor", "ch_vector", ev2)
                        if c == 0:
                            p.op("vector", lambda e, c=c: e.tensor_mul(
                                wacc[:, :], afull[:, :], bfull[c][:, :]))
                        else:
                            p.op("vector", lambda e, c=c: e.tensor_mul(
                                wtmp2[:, :], afull[:, :], bfull[c][:, :]))
                            p.op("vector", lambda e: e.tensor_add(
                                wacc[:, :], wacc[:, :], wtmp2[:, :]))
                    cv2 = p.op("vector", lambda e, ck=ck, wdst=wdst: e.tensor_copy(
                        wdst[:, ck * 512:(ck + 1) * 512], wacc[:, :]))
                    if var == "i":
                        p.op("vector", lambda e, ck=ck: e.tensor_scalar_mul(
                            wacc[:, :], wacc[:, :], -1.0))
                        cv2 = p.op("vector", lambda e, ck=ck: e.tensor_copy(
                            wineg_sb[:, ck * 512:(ck + 1) * 512], wacc[:, :]))

        with p.scope():
            _phaseD()

        if DEBUG_L1:
            for ck in range(4):
                p.wait("sync", "ch_vector", p.chv("vector"))
                p.dma("sync", wr_dbg[ck * 128:(ck + 1) * 128, :], wr_sb[:, ck * 512:(ck + 1) * 512])
                p.dma("sync", wi_dbg[ck * 128:(ck + 1) * 128, :], wi_sb[:, ck * 512:(ck + 1) * 512])

        def _phaseE():
            x1t = p.sb("x1tE", [128, D], F32)
            a1t = p.sb("a1tE", [128, D], BF16)
            outt = p.sb("outtE", [128, D], F32)
            pr = p.ps("prE", [128, 512])
            pi = p.ps("piE", [128, 512])
            stv = 0
            evv = 0
            for n2 in range(32):
                if evv:
                    p.wait("sync", "ch_vector", evv)
                lv1 = p.dma("sync", x1t[:, :], x1[n2:T:32, :])
                lv2 = p.dma("sync", a1t[:, :], attn1_tm[n2:T:32, :])
                if evv:
                    p.wait("tensor", "ch_vector", evv)
                for ck in range(4):
                    p.op("tensor", lambda e, ck=ck, n2=n2: e.matmul(
                        pr[:, :], xn1_sb[:, ck * T + n2:ck * T + n2 + 4065:32],
                        wr_sb[:, ck * 512:(ck + 1) * 512],
                        start=(ck == 0), stop=False))
                for ck in range(4):
                    p.op("tensor", lambda e, ck=ck, n2=n2: e.matmul(
                        pr[:, :], xn1_sb[:, (4 + ck) * T + n2:(4 + ck) * T + n2 + 4065:32],
                        wineg_sb[:, ck * 512:(ck + 1) * 512],
                        start=False, stop=False))
                p.op("tensor", lambda e: e.matmul(
                    pr[:, :], ones[:, :], bre_b[:, :], start=False, stop=True))
                for ck in range(4):
                    p.op("tensor", lambda e, ck=ck, n2=n2: e.matmul(
                        pi[:, :], xn1_sb[:, ck * T + n2:ck * T + n2 + 4065:32],
                        wi_sb[:, ck * 512:(ck + 1) * 512],
                        start=(ck == 0), stop=False))
                for ck in range(4):
                    p.op("tensor", lambda e, ck=ck, n2=n2: e.matmul(
                        pi[:, :], xn1_sb[:, (4 + ck) * T + n2:(4 + ck) * T + n2 + 4065:32],
                        wr_sb[:, ck * 512:(ck + 1) * 512],
                        start=False, stop=False))
                mvE = p.op("tensor", lambda e: e.matmul(
                    pi[:, :], ones[:, :], bim_b[:, :], start=False, stop=True),
                    extra_sem="l1mm")
                p.wait("vector", "l1mm", mvE)
                p.wait("vector", "dma_sync", lv2)
                if stv:
                    p.wait("vector", "pe_st", stv)
                p.op("vector", lambda e: e.tensor_add(outt[:, :], x1t[:, :],
                                                      a1t[:, :]))
                p.op("vector", lambda e: e.tensor_add(outt[:, 0:512], outt[:, 0:512],
                                                      pr[:, :]))
                evv = p.op("vector", lambda e: e.tensor_add(outt[:, 512:1024],
                                                            outt[:, 512:1024], pi[:, :]))
                p.wait("sync", "ch_vector", evv)
                stv = p.dma("sync", OUT[n2:T:32, :], outt[:, :], sem="pe_st")
        with p.scope():
            _phaseE()
        p.wait("sync", "dma_sync", p.val("dma_sync"))
        p.wait("sync", "pe_st", p.val("pe_st"))
        p.run_block()
    return nc, ["OUT"]



def build_xp():
    nc = mk_nc()
    xn_fm = nc.dram_tensor("xn_fm", [D, T], BF16, kind="ExternalInput").ap()
    Wp = nc.dram_tensor("Wp", [D, 4 * D], F32, kind="ExternalInput").ap()
    bp = nc.dram_tensor("bp", [1, 4 * D], F32, kind="ExternalInput").ap()
    onesc = nc.dram_tensor("onesc", [1, 128], BF16, kind="ExternalInput").ap()
    xp_bs = nc.dram_tensor("xp_bs", [4, 128, 32 * D], BF16, kind="ExternalOutput").ap()

    with ExitStack() as st:
        p = Plan(nc, st)
        ones = p.sb("ones", [1, 128], BF16)
        p.dma("sync", ones[:, :], onesc)
        xn_sb = p.sb("xn_sb", [128, 8 * T], BF16)
        for k in range(8):
            p.dma("sync", xn_sb[:, k * T:(k + 1) * T], xn_fm[k * 128:(k + 1) * 128, :])
        wtmp = p.sb("wtmp", [128, 512], F32)
        wp_sb = p.sb("wp_sb", [128, 8 * 512], BF16)
        brf = p.sb("brf", [1, 512], F32)
        brb = p.sb("brb", [1, 512], BF16)
        ot = p.sb("ot", [128, 512], BF16)
        pps = [p.ps("pxp%d" % b, [128, 512]) for b in range(2)]
        p.fence()
        p.wait("tensor", "dma_sync", p.val("dma_sync"))
        ev = {0: 0, 1: 0}
        stv = 0
        for s in range(4):
            for hh in range(2):
                for k in range(8):
                    lv = p.dma("sync", wtmp[:, :],
                               Wp[k * 128:(k + 1) * 128,
                                  s * D + hh * 512:s * D + hh * 512 + 512])
                    p.wait("scalar", "dma_sync", lv)
                    cv = p.op("scalar", lambda e, k=k: e.copy(
                        wp_sb[:, k * 512:(k + 1) * 512], wtmp[:, :]))
                    p.wait("sync", "ch_scalar", cv)
                bv = p.dma("sync", brf[:, :],
                           bp[:, s * D + hh * 512:s * D + hh * 512 + 512])
                p.wait("scalar", "dma_sync", bv)
                cb = p.op("scalar", lambda e: e.copy(brb[:, :], brf[:, :]))
                p.wait("tensor", "ch_scalar", cb)
                for n2 in range(32):
                    b = n2 % 2
                    if ev[b]:
                        p.wait("tensor", "ch_vector", ev[b])
                    for k in range(8):
                        p.op("tensor", lambda e, k=k, n2=n2, b=b: e.matmul(
                            pps[b][:, :],
                            xn_sb[:, k * T + n2:k * T + n2 + 4065:32],
                            wp_sb[:, k * 512:(k + 1) * 512],
                            start=(k == 0), stop=False))
                    p.op("tensor", lambda e, b=b: e.matmul(
                        pps[b][:, :], ones[:, :], brb[:, :],
                        start=False, stop=True), extra_sem="xpmm")
                    mv = p.val("xpmm")
                    p.wait("vector", "xpmm", mv)
                    if stv:
                        p.wait("vector", "xp_st", stv)
                    ev[b] = p.op("vector", lambda e, b=b: e.tensor_copy(
                        ot[:, :], pps[b][:, :]))
                    p.wait("sync", "ch_vector", ev[b])
                    stv = p.dma("sync",
                                xp_bs[s][:, n2 * D + hh * 512:n2 * D + hh * 512 + 512],
                                ot[:, :], sem="xp_st")
        p.wait("sync", "dma_sync", p.val("dma_sync"))
        p.wait("sync", "xp_st", p.val("xp_st"))
        p.run_block()
    return nc, ["xp_bs"]


def build_ln1t():
    nc = mk_nc()
    X = nc.dram_tensor("X", [T, D], F32, kind="ExternalInput").ap()
    attn0 = nc.dram_tensor("attn0", [T, D], BF16, kind="ExternalInput").ap()
    ffn0 = nc.dram_tensor("ffn0", [T, D], F32, kind="ExternalInput").ap()
    identc = nc.dram_tensor("identc", [128, 128], BF16, kind="ExternalInput").ap()
    x1 = nc.dram_tensor("x1", [T, D], F32, kind="ExternalOutput").ap()
    xn1_fm = nc.dram_tensor("xn1_fm", [D, T], BF16, kind="ExternalOutput").ap()
    xn1_tmp = nc.dram_tensor("xn1_tmp", [T, D], F32).ap()
    xn1_tb = nc.dram_tensor("xn1_tb", [T, D], BF16).ap()
    with ExitStack() as st:
        p = Plan(nc, st)
        ident = p.sb("ident", [128, 128], BF16)
        iv = p.dma("sync", ident[:, :], identc)
        with p.scope():
            emit_ln(p, [X, attn0, ffn0], xn1_tmp, x1, None, "ln1",
                    xs_dts=[F32, BF16, F32])
        with p.scope():
            emit_cast(p, xn1_tmp, xn1_tb, T, D, "c1")
        p.wait("tensor", "dma_sync", iv)
        with p.scope():
            emit_transpose(p, xn1_tb, xn1_fm, T, D, ident, "t1", src_dt=BF16)
        p.run_block()
    return nc, ["x1", "xn1_fm"]


def emit_fwd_fft(p, tag, ncg, ybuf, ch_stride, Cre, Cim, Xre, Xim,
                 d1re, d1im, bd_ap, bd_bufs, parts=("mm1", "mm2")):
    if "mm1" not in parts:
        pass
    else:
        _emit_fwd_mm1(p, tag, ncg, ybuf, ch_stride, Cre, Cim, d1re, d1im)
    if "mm2" not in parts:
        return
    _emit_fwd_mm2(p, tag, ncg, Cre, Cim, Xre, Xim, bd_ap, bd_bufs)


def _emit_fwd_mm1(p, tag, ncg, ybuf, ch_stride, Cre, Cim, d1re, d1im):
    pmre = [p.ps(tag + "pm1r%d" % b, [128, 512]) for b in range(2)]
    pmim = [p.ps(tag + "pm1i%d" % b, [128, 512]) for b in range(2)]
    msv = {0: 0, 1: 0}
    mev = {0: 0, 1: 0}
    for cg in range(ncg):
        b = cg % 2
        if mev[b]:
            p.wait("tensor", "ch_vector", mev[b])
            p.wait("tensor", "ch_scalar", msv[b])
        for c in range(4):
            base = 4 * cg + c
            lhs = ybuf[:, base:base + 31 * ch_stride + 1:ch_stride]
            tp = (0, 32 * c)
            p.op("tensor", lambda e, lhs=lhs, b=b, c=c, tp=tp: e.matmul(
                pmre[b][32 * c:32 * c + 32, 0:129], lhs, d1re[:, :],
                start=True, stop=True, tile_position=tp))
            p.op("tensor", lambda e, lhs=lhs, b=b, c=c, tp=tp: e.matmul(
                pmim[b][32 * c:32 * c + 32, 0:129], lhs, d1im[:, :],
                start=True, stop=True, tile_position=tp),
                extra_sem=(tag + "m1") if c == 3 else None)
        mv = p.val(tag + "m1")
        p.wait("scalar", tag + "m1", mv)
        p.wait("vector", tag + "m1", mv)
        msv[b] = p.op("scalar", lambda e, cg=cg, b=b: e.copy(
            Cre[:, cg * 129:(cg + 1) * 129], pmre[b][:, 0:129]))
        mev[b] = p.op("vector", lambda e, cg=cg, b=b: e.tensor_copy(
            Cim[:, cg * 129:(cg + 1) * 129], pmim[b][:, 0:129]))


def _emit_fwd_mm2(p, tag, ncg, Cre, Cim, Xre, Xim, bd_ap, bd_bufs):
    p.wait("tensor", "ch_scalar", p.chv("scalar"))
    p.wait("tensor", "ch_vector", p.chv("vector"))

    pxre = [p.ps(tag + "pm2r%d" % b, [128, 256]) for b in range(2)]
    pxim = [p.ps(tag + "pm2i%d" % b, [128, 256]) for b in range(2)]
    xsv = {0: 0, 1: 0}
    xev = {0: 0, 1: 0}
    for k1g in range(0, K1, 4):
        b = (k1g // 4) % 2
        nk = min(4, K1 - k1g)
        if xev[b]:
            p.wait("sync", "ch_vector", xev[b])
            p.wait("tensor", "ch_vector", xev[b])
            p.wait("tensor", "ch_scalar", xsv[b])
        for j in range(nk):
            k1 = k1g + j
            p.dma("sync", bd_bufs[b][:, j * 384:(j + 1) * 384],
                  bd_ap[k1], sem=tag + "bd%d" % b)
        p.wait("tensor", tag + "bd%d" % b, p.val(tag + "bd%d" % b))
        for j in range(nk):
            k1 = k1g + j
            bo = j * 384
            p.op("tensor", lambda e, k1=k1, b=b, j=j, bo=bo: e.matmul(
                pxre[b][:, j * 64:j * 64 + ncg], bd_bufs[b][:, bo:bo + 128],
                Cre[:, k1::129], start=True, stop=False))
            p.op("tensor", lambda e, k1=k1, b=b, j=j, bo=bo: e.matmul(
                pxre[b][:, j * 64:j * 64 + ncg], bd_bufs[b][:, bo + 256:bo + 384],
                Cim[:, k1::129], start=False, stop=True))
            p.op("tensor", lambda e, k1=k1, b=b, j=j, bo=bo: e.matmul(
                pxim[b][:, j * 64:j * 64 + ncg],
                bd_bufs[b][:, bo + 128:bo + 256],
                Cre[:, k1::129], start=True, stop=False))
            p.op("tensor", lambda e, k1=k1, b=b, j=j, bo=bo: e.matmul(
                pxim[b][:, j * 64:j * 64 + ncg], bd_bufs[b][:, bo:bo + 128],
                Cim[:, k1::129], start=False, stop=True),
                extra_sem=(tag + "m2") if j == nk - 1 else None)
        mv = p.val(tag + "m2")
        p.wait("scalar", tag + "m2", mv)
        p.wait("vector", tag + "m2", mv)
        xsv[b] = p.op("scalar", lambda e, k1g=k1g, b=b, nk=nk: e.copy(
            Xre[:, k1g * ncg:(k1g + nk) * ncg].rearrange("p (j x) -> p j x", j=nk),
            pxre[b].ap().rearrange("p (j x) -> p j x", j=4)[:, 0:nk, 0:ncg]))
        xev[b] = p.op("vector", lambda e, k1g=k1g, b=b, nk=nk: e.tensor_copy(
            Xim[:, k1g * ncg:(k1g + nk) * ncg].rearrange("p (j x) -> p j x", j=nk),
            pxim[b].ap().rearrange("p (j x) -> p j x", j=4)[:, 0:nk, 0:ncg]))


def emit_filter(p, tag, quads, tmp1, tmp2):
    for (xr, xi, fr, fi, n) in quads:
        t1 = tmp1[:, 0:n].rearrange("p (k g) -> p k g", g=32) \
            if len(xr.shape) == 3 else tmp1[:, 0:n]
        t2 = tmp2[:, 0:n].rearrange("p (k g) -> p k g", g=32) \
            if len(xr.shape) == 3 else tmp2[:, 0:n]
        p.op("vector", lambda e, xr=xr, fr=fr, t1=t1: e.tensor_mul(t1, xr, fr))
        p.op("vector", lambda e, xr=xr, fi=fi, t2=t2: e.tensor_mul(t2, xr, fi))
        p.op("vector", lambda e, xr=xr, xi=xi, fi=fi: e.tensor_mul(xr, xi, fi))
        p.op("vector", lambda e, xr=xr, t1=t1: e.tensor_sub(xr, t1, xr))
        p.op("vector", lambda e, xi=xi, fr=fr: e.tensor_mul(xi, xi, fr))
        p.op("vector", lambda e, xi=xi, t2=t2: e.tensor_add(xi, xi, t2))


def emit_inv_fft(p, tag, ncg, Vre, Vim, iwre, iwim, iwnim,
                 SAre, SAim, SBre, SBim, ea_re, ea_nim, eb_re, eb_nim,
                 gbuf, outbuf):
    NC4 = 4 * ncg
    ps2a = [p.ps(tag + "pi2a%d" % b, [128, 256]) for b in range(2)]
    ps2b = [p.ps(tag + "pi2b%d" % b, [128, 256]) for b in range(2)]
    ssv = {0: 0, 1: 0}
    sev = {0: 0, 1: 0}
    for cg in range(ncg):
        b = cg % 2
        if sev[b]:
            p.wait("tensor", "ch_vector", sev[b])
            p.wait("tensor", "ch_scalar", ssv[b])
        for dst, k0, kn in ((ps2a, 0, K1A), (ps2b, K1A, K1B)):
            lhsr = Vre[:, cg::ncg][:, k0:k0 + kn]
            lhsi = Vim[:, cg::ncg][:, k0:k0 + kn]
            p.op("tensor", lambda e, lhsr=lhsr, b=b, dst=dst, kn=kn: e.matmul(
                dst[b][0:kn, 0:128], lhsr, iwre[:, :],
                start=True, stop=False))
            p.op("tensor", lambda e, lhsi=lhsi, b=b, dst=dst, kn=kn: e.matmul(
                dst[b][0:kn, 0:128], lhsi, iwnim[:, :],
                start=False, stop=True))
            p.op("tensor", lambda e, lhsr=lhsr, b=b, dst=dst, kn=kn: e.matmul(
                dst[b][0:kn, 128:256], lhsr, iwim[:, :],
                start=True, stop=False))
            p.op("tensor", lambda e, lhsi=lhsi, b=b, dst=dst, kn=kn: e.matmul(
                dst[b][0:kn, 128:256], lhsi, iwre[:, :],
                start=False, stop=True),
                extra_sem=(tag + "i2") if k0 == K1A else None)
        mv = p.val(tag + "i2")
        p.wait("scalar", tag + "i2", mv)
        p.wait("vector", tag + "i2", mv)
        dstA_re = SAre.rearrange("p (n c) -> p c n", c=NC4)[:, 4 * cg:4 * cg + 4, :]
        dstA_im = SAim.rearrange("p (n c) -> p c n", c=NC4)[:, 4 * cg:4 * cg + 4, :]
        dstB_re = SBre.ap().rearrange("p (n c) -> p c n", c=NC4)[:, 4 * cg:4 * cg + 4, :]
        dstB_im = SBim.ap().rearrange("p (n c) -> p c n", c=NC4)[:, 4 * cg:4 * cg + 4, :]
        p.op("scalar", lambda e, dstA_re=dstA_re, b=b: e.copy(
            dstA_re, ps2a[b][0:K1A, 0:128].rearrange("p (c n) -> p c n", c=4)))
        ssv[b] = p.op("scalar", lambda e, dstA_im=dstA_im, b=b: e.copy(
            dstA_im, ps2a[b][0:K1A, 128:256].rearrange("p (c n) -> p c n", c=4)))
        p.op("vector", lambda e, dstB_re=dstB_re, b=b: e.tensor_copy(
            dstB_re, ps2b[b][0:K1B, 0:128].rearrange("p (c n) -> p c n", c=4)))
        sev[b] = p.op("vector", lambda e, dstB_im=dstB_im, b=b: e.tensor_copy(
            dstB_im, ps2b[b][0:K1B, 128:256].rearrange("p (c n) -> p c n", c=4)))

    p.wait("tensor", "ch_scalar", p.chv("scalar"))
    p.wait("tensor", "ch_vector", p.chv("vector"))

    ps1 = [p.ps(tag + "pi1%d" % b, [128, 512]) for b in range(2)]
    oev = {0: 0, 1: 0}
    for n2 in range(32):
        b = n2 % 2
        if oev[b]:
            p.wait("tensor", "ch_vector", oev[b])
        ea_r = ea_re[:, n2:T:32]
        ea_n = ea_nim[:, n2:T:32]
        eb_r = eb_re[:, n2:T:32]
        eb_n = eb_nim[:, n2:T:32]
        p.op("tensor", lambda e, b=b, ea_r=ea_r, n2=n2: e.matmul(
            ps1[b][:, 0:NC4], ea_r, SAre[:, n2 * NC4:(n2 + 1) * NC4],
            start=True, stop=False))
        p.op("tensor", lambda e, b=b, ea_n=ea_n, n2=n2: e.matmul(
            ps1[b][:, 0:NC4], ea_n, SAim[:, n2 * NC4:(n2 + 1) * NC4],
            start=False, stop=False))
        p.op("tensor", lambda e, b=b, eb_r=eb_r, n2=n2: e.matmul(
            ps1[b][:, 0:NC4], eb_r, SBre[:, n2 * NC4:(n2 + 1) * NC4],
            start=False, stop=False))
        p.op("tensor", lambda e, b=b, eb_n=eb_n, n2=n2: e.matmul(
            ps1[b][:, 0:NC4], eb_n, SBim[:, n2 * NC4:(n2 + 1) * NC4],
            start=False, stop=True), extra_sem=tag + "i1")
        mv = p.val(tag + "i1")
        p.wait("vector", tag + "i1", mv)
        oev[b] = p.op("vector", lambda e, n2=n2, b=b: e.tensor_mul(
            outbuf[:, n2 * NC4:(n2 + 1) * NC4], ps1[b][:, 0:NC4],
            gbuf[:, n2 * NC4:(n2 + 1) * NC4]))


def _conv_quarter(p, tag, s, q, last, y_src_ap, g_src_ap, Fall, out_ap,
                  consts_sb, bd_ap):
    ncg = 64
    NC4 = 256
    d1re, d1im, iwre, iwim, iwnim, ea_re, ea_nim, eb_re, eb_nim = consts_sb
    ybuf = p.sb(tag + "y", [128, 32 * NC4], BF16)
    gbuf = p.sb(tag + "g", [128, 32 * NC4], BF16)
    outb = ybuf
    Cre = p.sb(tag + "cre", [128, 129 * ncg], BF16)
    Cim = p.sb(tag + "cim", [128, 129 * ncg], BF16)
    Xre = p.sb(tag + "xre", [128, 129 * ncg], BF16)
    Xim = p.sb(tag + "xim", [128, 129 * ncg], BF16)
    Fre = p.sb(tag + "fre", [128, 2 * 129 * 32], BF16)
    Fim = p.sb(tag + "fim", [128, 2 * 129 * 32], BF16)
    SBre = p.sb(tag + "sbre", [64, 32 * NC4], BF16)
    SBim = p.sb(tag + "sbim", [64, 32 * NC4], BF16)
    tmp1 = p.sb(tag + "t1", [128, 33 * 32], BF16)
    tmp2 = p.sb(tag + "t2", [128, 33 * 32], BF16)
    bd_bufs = [p.sb(tag + "bd%d" % b, [128, 4 * 384], BF16) for b in range(2)]
    SAre = Cre[0:65, 0:32 * NC4]
    SAim = Cim[0:65, 0:32 * NC4]

    for n2 in range(32):
        p.dma("sync", ybuf[:, n2 * NC4:(n2 + 1) * NC4],
              y_src_ap[:, n2 * D + q * NC4:n2 * D + q * NC4 + NC4])
        p.dma("sync", gbuf[:, n2 * NC4:(n2 + 1) * NC4],
              g_src_ap[:, n2 * D + q * NC4:n2 * D + q * NC4 + NC4])
    for rr in range(2):
        rank = 2 * q + rr
        p.dma("sync", Fre[:, rr * 4128:(rr + 1) * 4128], Fall[rank, s, 0])
        p.dma("sync", Fim[:, rr * 4128:(rr + 1) * 4128], Fall[rank, s, 1])
    p.fence()
    p.wait("tensor", "dma_sync", p.val("dma_sync"))
    p.wait("vector", "dma_sync", p.val("dma_sync"))
    p.wait("scalar", "dma_sync", p.val("dma_sync"))

    with p.scope():
        emit_fwd_fft(p, tag, ncg, ybuf.ap(), NC4, Cre, Cim, Xre, Xim, d1re, d1im,
                     bd_ap, bd_bufs)
    p.wait("vector", "ch_scalar", p.chv("scalar"))
    quads = []
    for rr in range(2):
        for kq in range(0, K1, 33):
            kn = min(33, K1 - kq)
            xr = Xre.ap().rearrange("p (k g) -> p k g", g=64)[
                :, kq:kq + kn, rr * 32:(rr + 1) * 32]
            xi = Xim.ap().rearrange("p (k g) -> p k g", g=64)[
                :, kq:kq + kn, rr * 32:(rr + 1) * 32]
            fr = Fre[:, rr * 4128:(rr + 1) * 4128].rearrange(
                "p (k g) -> p k g", g=32)[:, kq:kq + kn, :]
            fi = Fim[:, rr * 4128:(rr + 1) * 4128].rearrange(
                "p (k g) -> p k g", g=32)[:, kq:kq + kn, :]
            quads.append((xr, xi, fr, fi, kn * 32))
    emit_filter(p, tag, quads, tmp1, tmp2)
    p.wait("tensor", "ch_vector", p.chv("vector"))
    p.wait("scalar", "ch_vector", p.chv("vector"))
    emit_inv_fft(p, tag, ncg, Xre, Xim, iwre, iwim, iwnim,
                 SAre, SAim, SBre, SBim, ea_re, ea_nim, eb_re, eb_nim,
                 gbuf, outb)
    p.wait("sync", "ch_vector", p.chv("vector"))
    for n2 in range(32):
        if last:
            p.dma("sync", out_ap[n2:T:32, q * NC4:(q + 1) * NC4],
                  outb[:, n2 * NC4:(n2 + 1) * NC4], sem="cv_st")
        else:
            p.dma("sync", out_ap[:, n2 * D + q * NC4:n2 * D + q * NC4 + NC4],
                  outb[:, n2 * NC4:(n2 + 1) * NC4], sem="cv_st")
    p.wait("sync", "cv_st", p.val("cv_st"))


def _load_conv_consts(p, cl):
    d1re = p.sb("d1re", [128, 129], BF16)
    d1im = p.sb("d1im", [128, 129], BF16)
    iwre = p.sb("iwre", [128, 128], BF16)
    iwim = p.sb("iwim", [128, 128], BF16)
    iwnim = p.sb("iwnim", [128, 128], BF16)
    ea_re = p.sb("ea_re", [65, T], BF16)
    ea_nim = p.sb("ea_nim", [65, T], BF16)
    eb_re = p.sb("eb_re", [64, T], BF16)
    eb_nim = p.sb("eb_nim", [64, T], BF16)
    for nm, t in (("d1rec", d1re), ("d1imc", d1im), ("iwrec", iwre),
                  ("iwimc", iwim), ("iwnimc", iwnim), ("earec", ea_re),
                  ("eanimc", ea_nim), ("ebrec", eb_re), ("ebnimc", eb_nim)):
        p.dma("sync", t[:, :], cl[nm])
    return (d1re, d1im, iwre, iwim, iwnim, ea_re, ea_nim, eb_re, eb_nim)


def _declare_conv_consts(nc):
    cl = {}
    cl["d1rec"] = nc.dram_tensor("d1rec", [128, 129], BF16, kind="ExternalInput").ap()
    cl["d1imc"] = nc.dram_tensor("d1imc", [128, 129], BF16, kind="ExternalInput").ap()
    cl["iwrec"] = nc.dram_tensor("iwrec", [128, 128], BF16, kind="ExternalInput").ap()
    cl["iwimc"] = nc.dram_tensor("iwimc", [128, 128], BF16, kind="ExternalInput").ap()
    cl["iwnimc"] = nc.dram_tensor("iwnimc", [128, 128], BF16, kind="ExternalInput").ap()
    cl["earec"] = nc.dram_tensor("earec", [65, T], BF16, kind="ExternalInput").ap()
    cl["eanimc"] = nc.dram_tensor("eanimc", [65, T], BF16, kind="ExternalInput").ap()
    cl["ebrec"] = nc.dram_tensor("ebrec", [64, T], BF16, kind="ExternalInput").ap()
    cl["ebnimc"] = nc.dram_tensor("ebnimc", [64, T], BF16, kind="ExternalInput").ap()
    cl["bdc"] = nc.dram_tensor("bdc", [K1, 128, 384], BF16,
                               kind="ExternalInput").ap()
    return cl


def build_conv_stage(s):
    last = (s == 2)

    def build():
        nc = mk_nc()
        y_in = nc.dram_tensor("y_in", [128, 32 * D], BF16, kind="ExternalInput").ap()
        xp_g = nc.dram_tensor("xp_g", [128, 32 * D], BF16, kind="ExternalInput").ap()
        Fall = nc.dram_tensor("F_all", [8, 3, 2, 128, 32 * 129], BF16,
                              kind="ExternalInput").ap()
        cl = _declare_conv_consts(nc)
        if last:
            out = nc.dram_tensor("attn0", [T, D], BF16, kind="ExternalOutput").ap()
        else:
            out = nc.dram_tensor("y_out", [128, 32 * D], BF16,
                                 kind="ExternalOutput").ap()
        with ExitStack() as st:
            p = Plan(nc, st)
            consts_sb = _load_conv_consts(p, cl)
            for q in range(4):
                with p.scope():
                    _conv_quarter(p, "cq%d" % q, s, q, last, y_in, xp_g, Fall,
                                  out, consts_sb, cl["bdc"])
            p.wait("sync", "dma_sync", p.val("dma_sync"))
            p.run_block()
        return nc, ["attn0" if last else "y_out"]
    return build


def build_fprep():
    nc = mk_nc()
    pos_fm = nc.dram_tensor("pos_fm", [D, T], BF16, kind="ExternalInput").ap()
    Wf_sh = nc.dram_tensor("Wf_shard", [D, 3 * 128], F32, kind="ExternalInput").ap()
    bf_sh = nc.dram_tensor("bf_shard", [1, 3 * 128], F32, kind="ExternalInput").ap()
    onesc = nc.dram_tensor("onesc", [1, 128], BF16, kind="ExternalInput").ap()
    cl = _declare_conv_consts(nc)
    Fsh = nc.dram_tensor("F_shard", [3, 2, 128, 32 * 129], BF16,
                         kind="ExternalOutput").ap()
    with ExitStack() as st:
        p = Plan(nc, st)
        consts_sb = _load_conv_consts(p, cl)
        d1re, d1im = consts_sb[0], consts_sb[1]
        ones = p.sb("ones", [1, 128], BF16)
        p.dma("sync", ones[:, :], onesc)
        pos_sb = p.sb("pos_sb", [128, 8 * T], BF16)
        for k in range(8):
            p.dma("sync", pos_sb[:, k * T:(k + 1) * T],
                  pos_fm[k * 128:(k + 1) * 128, :])
        ncg = 32
        for s in range(3):
            with p.scope():
                tag = "fp%d" % s
                wtmp = p.sb(tag + "wtmp", [128, 128], F32)
                wf_sb = p.sb(tag + "wf", [128, 8 * 128], BF16)
                brf = p.sb(tag + "brf", [1, 128], F32)
                brb = p.sb(tag + "brb", [1, 128], BF16)
                fbuf = p.sb(tag + "fb", [128, 32 * 128], BF16)
                Cre = p.sb(tag + "cre", [128, 129 * ncg], BF16)
                Cim = p.sb(tag + "cim", [128, 129 * ncg], BF16)
                Xre = p.sb(tag + "xre", [128, 129 * ncg], BF16)
                Xim = p.sb(tag + "xim", [128, 129 * ncg], BF16)
                bd_bufs = [p.sb(tag + "bd%d" % b, [128, 4 * 384], BF16)
                           for b in range(2)]
                for k in range(8):
                    lv = p.dma("sync", wtmp[:, :],
                               Wf_sh[k * 128:(k + 1) * 128, s * 128:(s + 1) * 128])
                    p.wait("scalar", "dma_sync", lv)
                    cv = p.op("scalar", lambda e, k=k, wf_sb=wf_sb, wtmp=wtmp:
                              e.copy(wf_sb[:, k * 128:(k + 1) * 128], wtmp[:, :]))
                    p.wait("sync", "ch_scalar", cv)
                bv = p.dma("sync", brf[:, :], bf_sh[:, s * 128:(s + 1) * 128])
                p.wait("scalar", "dma_sync", bv)
                cb = p.op("scalar", lambda e, brb=brb, brf=brf:
                          e.copy(brb[:, :], brf[:, :]))
                p.wait("tensor", "ch_scalar", cb)
                pscope = p.scope(); pscope.__enter__()
                pf = [p.ps(tag + "pf%d" % b, [128, 128]) for b in range(2)]
                fev = {0: 0, 1: 0}
                for n2 in range(32):
                    b = n2 % 2
                    if fev[b]:
                        p.wait("tensor", "ch_vector", fev[b])
                    for k in range(8):
                        p.op("tensor", lambda e, k=k, n2=n2, b=b, pf=pf,
                             wf_sb=wf_sb: e.matmul(
                            pf[b][:, :], pos_sb[:, k * T + n2:k * T + n2 + 4065:32],
                            wf_sb[:, k * 128:(k + 1) * 128],
                            start=(k == 0), stop=False))
                    p.op("tensor", lambda e, b=b, pf=pf, brb=brb: e.matmul(
                        pf[b][:, :], ones[:, :], brb[:, :], start=False, stop=True),
                        extra_sem=tag + "fm")
                    mv = p.val(tag + "fm")
                    p.wait("vector", tag + "fm", mv)
                    fev[b] = p.op("vector", lambda e, n2=n2, b=b, pf=pf, fbuf=fbuf:
                                  e.tensor_copy(
                        fbuf[:, n2 * 128:(n2 + 1) * 128], pf[b][:, :]))
                p.wait("tensor", "ch_vector", p.chv("vector"))
                pscope.__exit__(None, None, None)
                with p.scope():
                    emit_fwd_fft(p, tag, ncg, fbuf.ap(), 128, Cre, Cim, Xre, Xim,
                                 d1re, d1im, cl["bdc"], bd_bufs)
                p.wait("sync", "ch_vector", p.chv("vector"))
                p.wait("sync", "ch_scalar", p.chv("scalar"))
                p.dma("sync", Fsh[s, 0], Xre[:, :], sem="fs_st")
                p.dma("sync", Fsh[s, 1], Xim[:, :], sem="fs_st")
        p.wait("sync", "dma_sync", p.val("dma_sync"))
        p.wait("sync", "fs_st", p.val("fs_st"))
        p.run_block()
    return nc, ["F_shard"]


def _spmd(build_fn, in_maps):
    nc, out_names = build_fn()
    res = run_bass_kernel_spmd(nc, in_maps, core_ids=list(range(len(in_maps))))
    return res.results


def kernel(**inputs):
    C = make_consts()
    X = np.asarray(inputs["L0_attn_X"], np.float32)
    Wp = np.ascontiguousarray(np.asarray(inputs["L0_attn_Wp"], np.float32))
    bp = np.asarray(inputs["L0_attn_bp"], np.float32).reshape(1, 4 * D)
    Wf = np.ascontiguousarray(np.asarray(inputs["L0_attn_Wf"], np.float32))
    bfv = np.asarray(inputs["L0_attn_bf"], np.float32)
    clip_x = np.asarray(inputs["L0_ffn_clip_x"], np.float32).reshape(1, 1)
    clip_down = np.asarray(inputs["L0_ffn_clip_down"], np.float32).reshape(1, 1)
    w_gate = np.ascontiguousarray(np.asarray(inputs["L0_ffn_w_gate"], np.float32))
    w_up = np.ascontiguousarray(np.asarray(inputs["L0_ffn_w_up"], np.float32))
    w_down = np.ascontiguousarray(np.asarray(inputs["L0_ffn_w_down"], np.float32))

    ident = np.asarray(C["ident"])
    identf = np.asarray(C["identf"])
    ones128 = np.asarray(C["ones128"])
    ones512 = np.asarray(C["ones512"])

    conv_consts = {
        "d1rec": np.asarray(C["D1re"]), "d1imc": np.asarray(C["D1im"]),
        "iwrec": np.asarray(C["iWre"]), "iwimc": np.asarray(C["iWim"]),
        "iwnimc": np.asarray(C["iWnegim"]),
        "earec": np.asarray(C["EreA"]), "eanimc": np.asarray(C["EnimA"]),
        "ebrec": np.asarray(C["EreB"]), "ebnimc": np.asarray(C["EnimB"]),
        "bdc": np.asarray(C["BDcat"])}

    r = _spmd(build_ln0, [{"X": X[c]} for c in range(NCORES)])
    xn_f32 = [r[c]["xn_f32"] for c in range(NCORES)]
    pmax_all = np.ascontiguousarray(
        np.stack([r[c]["pmax"][:, 0] for c in range(NCORES)]))

    r = _spmd(build_qt, [{"xn_f32": xn_f32[c], "pmax_all": pmax_all,
                          "clip_x": clip_x, "identc": ident}
                         for c in range(NCORES)])
    xn_fm = [r[c]["xn_fm"] for c in range(NCORES)]
    xq_fm = [r[c]["xq_fm"] for c in range(NCORES)]
    scale_x = [r[c]["scale_x"] for c in range(NCORES)]

    r = _spmd(build_ffn0a, [{"xq_fm": xq_fm[c], "scale_x": scale_x[c],
                             "w_gate": w_gate, "w_up": w_up, "identc": ident}
                            for c in range(NCORES)])
    h_f32 = [r[c]["h_f32"] for c in range(NCORES)]
    pmax_h_all = np.ascontiguousarray(
        np.stack([r[c]["pmax_h"][:, 0] for c in range(NCORES)]))

    r = _spmd(build_ffn0b, [{"h_f32": h_f32[c], "pmax_h_all": pmax_h_all,
                             "clip_down": clip_down, "w_down": w_down,
                             "identc": ident} for c in range(NCORES)])
    ffn0 = [r[c]["ffn0"] for c in range(NCORES)]

    r = _spmd(build_xp, [{"xn_fm": xn_fm[c], "Wp": Wp, "bp": bp,
                          "onesc": ones128} for c in range(NCORES)])
    xp_bs = [r[c]["xp_bs"] for c in range(NCORES)]

    fmaps = []
    for c in range(NCORES):
        cols = np.concatenate([np.arange(s * D + 128 * c, s * D + 128 * c + 128)
                               for s in range(3)])
        fmaps.append(dict(conv_consts, pos_fm=np.asarray(C["pos_fm"]),
                          Wf_shard=np.ascontiguousarray(Wf[:, cols]),
                          bf_shard=np.ascontiguousarray(bfv[cols]).reshape(1, 384),
                          onesc=ones128))
    r = _spmd(build_fprep, fmaps)
    F_all = np.ascontiguousarray(
        np.stack([r[c]["F_shard"] for c in range(NCORES)]))

    y = [xp_bs[c][0] for c in range(NCORES)]
    for s in range(3):
        r = _spmd(build_conv_stage(s),
                  [dict(conv_consts, y_in=y[c], xp_g=xp_bs[c][s + 1],
                        F_all=F_all) for c in range(NCORES)])
        if s < 2:
            y = [r[c]["y_out"] for c in range(NCORES)]
    attn0 = [r[c]["attn0"] for c in range(NCORES)]

    r = _spmd(build_ln1t, [{"X": X[c], "attn0": attn0[c], "ffn0": ffn0[c],
                            "identc": ident} for c in range(NCORES)])
    x1 = [r[c]["x1"] for c in range(NCORES)]
    xn1_fm = [r[c]["xn1_fm"] for c in range(NCORES)]

    l1_common = {
        "W_zf": np.ascontiguousarray(np.asarray(inputs["L1_attn_W_zf"], np.float32)),
        "b_zf": np.asarray(inputs["L1_attn_b_zf"], np.float32).reshape(512, 1),
        "W_hf": np.ascontiguousarray(np.asarray(inputs["L1_attn_W_hf"], np.float32)),
        "b_hf": np.asarray(inputs["L1_attn_b_hf"], np.float32).reshape(512, 1),
        "W_zs": np.ascontiguousarray(np.asarray(inputs["L1_attn_W_zs"], np.float32)),
        "b_zs": np.asarray(inputs["L1_attn_b_zs"], np.float32).reshape(512, 1),
        "W_hs": np.ascontiguousarray(np.asarray(inputs["L1_attn_W_hs"], np.float32)),
        "b_hs": np.asarray(inputs["L1_attn_b_hs"], np.float32).reshape(512, 1),
        "W_g": np.ascontiguousarray(np.asarray(inputs["L1_attn_W_g"], np.float32)),
        "b_g": np.asarray(inputs["L1_attn_b_g"], np.float32).reshape(D, 1),
        "G1_R": np.asarray(inputs["L1_ffn_G1_R"], np.float32).reshape(128, 32),
        "G2_R": np.asarray(inputs["L1_ffn_G2_R"], np.float32).reshape(64, 16),
        "G1_I": np.asarray(inputs["L1_ffn_G1_I"], np.float32).reshape(128, 32),
        "G2_I": np.asarray(inputs["L1_ffn_G2_I"], np.float32).reshape(64, 16),
        "bias_real": np.asarray(inputs["L1_ffn_bias_real"], np.float32).reshape(1, 512),
        "bias_imag": np.asarray(inputs["L1_ffn_bias_imag"], np.float32).reshape(1, 512),
        "onesc": ones128, "ones512c": ones512,
        "repac": np.asarray(C["REPA"]), "perioc": np.asarray(C["PERIO"]),
        "identc": ident, "identf": identf}
    r = _spmd(build_l1, [dict(l1_common, xn1_fm=xn1_fm[c], x1=x1[c])
                         for c in range(NCORES)])
    out = np.stack([r[c]["OUT"] for c in range(NCORES)]).astype(np.float32)
    return out


# version 04
# speedup vs baseline: 1.3666x; 1.3666x over previous
import math
from contextlib import ExitStack

import numpy as np
import ml_dtypes

import concourse.bass as bass
import concourse.mybir as mybir
from concourse.bass_utils import run_bass_kernel_spmd

F32 = mybir.dt.float32
BF16 = mybir.dt.bfloat16
I32 = mybir.dt.int32
AF = mybir.ActivationFunctionType
ALU = mybir.AluOpType
AXX = mybir.AxisListType.X

BF = ml_dtypes.bfloat16

NCORES = 8
T = 4096
D = 1024
L = 8192
N1 = 256
N2 = 32
K1 = 129
K1A, K1B = 65, 64
STAGES = 3
EPS = 1e-5
QMAX = 127.0
DI = 2816

ENGINES = ("sync", "vector", "scalar", "tensor", "gpsimd")
DEBUG_L1 = False



_CONSTS = None


def make_consts():
    global _CONSTS
    if _CONSTS is not None:
        return _CONSTS
    c = {}

    pos_t = np.arange(T, dtype=np.float64)[:, None]
    i = np.arange(D, dtype=np.float64)[None, :]
    angle = pos_t / np.power(10000.0, (2 * (i // 2)) / D)
    pos = np.where(i % 2 == 0, np.sin(angle), np.cos(angle))
    c["pos_fm"] = np.ascontiguousarray(pos.T).astype(BF)

    n1 = np.arange(128, dtype=np.float64)[:, None]
    k1 = np.arange(K1, dtype=np.float64)[None, :]
    th = 2.0 * np.pi * n1 * k1 / N1
    c["D1re"] = np.cos(th).astype(BF)
    c["D1im"] = (-np.sin(th)).astype(BF)

    n2 = np.arange(N2, dtype=np.float64)
    k2 = np.arange(N2, dtype=np.float64)
    sre = np.zeros((K1, 128, 128), np.float64)
    sim = np.zeros((K1, 128, 128), np.float64)
    for kk in range(K1):
        ph = (2.0 * np.pi * n2 * kk / L)[:, None] \
            + 2.0 * np.pi * n2[:, None] * k2[None, :] / N2
        br, bi = np.cos(ph), -np.sin(ph)
        for cb in range(4):
            sl = slice(32 * cb, 32 * cb + 32)
            sre[kk][sl, sl] = br
            sim[kk][sl, sl] = bi
    c["BDre"] = sre.astype(BF)
    c["BDim"] = sim.astype(BF)
    c["BDnegim"] = (-sim).astype(BF)
    c["BDcat"] = np.concatenate([sre, sim, -sim], axis=2).astype(BF)

    ph = 2.0 * np.pi * k2[:, None] * n2[None, :] / N2
    iwre = np.zeros((128, 128), np.float64)
    iwim = np.zeros((128, 128), np.float64)
    for cb in range(4):
        sl = slice(32 * cb, 32 * cb + 32)
        iwre[sl, sl] = np.cos(ph)
        iwim[sl, sl] = np.sin(ph)
    c["iWre"] = iwre.astype(BF)
    c["iWim"] = iwim.astype(BF)
    c["iWnegim"] = (-iwim).astype(BF)

    g = np.full(K1, 2.0)
    g[0] = 1.0
    g[128] = 1.0
    kk = np.arange(K1, dtype=np.float64)[:, None]
    tt = np.arange(T, dtype=np.float64)[None, :]
    ph = 2.0 * np.pi * kk * tt / L
    ere = (g[:, None] / L) * np.cos(ph)
    eim = (g[:, None] / L) * np.sin(ph)
    c["EreA"] = ere[:K1A].astype(BF)
    c["EnimA"] = (-eim[:K1A]).astype(BF)
    c["EreB"] = ere[K1A:].astype(BF)
    c["EnimB"] = (-eim[K1A:]).astype(BF)

    c["ident"] = np.eye(128).astype(BF)
    c["identf"] = np.eye(128).astype(np.float32)
    c["ones128"] = np.ones((1, 128)).astype(BF)
    c["ones512"] = np.ones((1, 512)).astype(BF)
    repa = np.zeros((32, 512), np.float32)
    for ck in range(4):
        for d8 in range(8):
            for g in range(16):
                repa[ck * 8 + d8, ck * 128 + d8 * 16 + g] = 1.0
    c["REPA"] = repa
    perio = np.zeros((16, 128), np.float32)
    for d8 in range(8):
        for g in range(16):
            perio[g, d8 * 16 + g] = 1.0
    c["PERIO"] = perio

    _CONSTS = c
    return c



class Plan:

    CHAINED = ("vector", "scalar", "gpsimd")

    def __init__(self, nc, stack):
        self.nc = nc
        self.stack = stack
        self.scopes = [stack]
        self.th = {e: [] for e in ENGINES}
        self._sems = {}

    def sb(self, name, shape, dt):
        return self.scopes[-1].enter_context(self.nc.sbuf_tensor(name, shape, dt))

    def ps(self, name, shape, dt=F32):
        return self.scopes[-1].enter_context(self.nc.psum_tensor(name, shape, dt))

    def full_barrier(self):
        bh, bv = self.inc(self.sem("barrier"), 1)
        self.emit("tensor", lambda e, bh=bh: e.nop().then_inc(bh, 1))
        targets = [("barrier", bv)]
        for nm, s in list(self._sems.items()):
            if nm != "barrier" and s[1] > 0:
                targets.append((nm, s[1]))
        for eng in ENGINES:
            for nm, v in targets:
                self.wait(eng, nm, v)

    class _Scope:
        def __init__(self, p):
            self.p = p

        def __enter__(self):
            es = ExitStack()
            self.p.scopes.append(es)
            return es

        def __exit__(self, *a):
            self.p.full_barrier()
            es = self.p.scopes.pop()
            es.close()
            return False

    def scope(self):
        return Plan._Scope(self)

    def sem(self, name):
        if name not in self._sems:
            h = self.stack.enter_context(self.nc.semaphore(name))
            self._sems[name] = [h, 0]
        return name

    def inc(self, name, k):
        s = self._sems[self.sem(name)]
        s[1] += k
        return s[0], s[1]

    def val(self, name):
        return self._sems[self.sem(name)][1]

    def handle(self, name):
        return self._sems[self.sem(name)][0]

    def emit(self, eng, fn):
        self.th[eng].append(fn)

    def dma(self, eng, out, in_, sem=None, **kw):
        nm = sem if sem is not None else ("dma_" + eng)
        h, v = self.inc(nm, 16)
        self.emit(eng, lambda e, out=out, in_=in_, h=h, kw=kw:
                  e.dma_start(out=out, in_=in_, **kw).then_inc(h, 16))
        return v

    def fence(self):
        self.wait("sync", "dma_sync", self.val(self.sem("dma_sync")))

    def wait(self, eng, sem_name, value=None):
        v = self.val(sem_name) if value is None else value
        h = self.handle(sem_name)
        self.emit(eng, lambda e, h=h, v=v: e.wait_ge(h, v))

    def op(self, eng, fn, extra_sem=None, extra_k=1):
        if eng in self.CHAINED:
            assert extra_sem is None
            nm = "ch_" + eng
            cur = self.val(self.sem(nm))
            h, v = self.inc(nm, 1)

            def thunk(e, fn=fn, h=h, cur=cur):
                if cur:
                    e.wait_ge(h, cur)
                fn(e).then_inc(h, 1)
            self.emit(eng, thunk)
            return v
        else:
            if extra_sem is None:
                self.emit(eng, fn)
                return None
            h, v = self.inc(extra_sem, extra_k)
            self.emit(eng, lambda e, fn=fn, h=h, k=extra_k: fn(e).then_inc(h, k))
            return v

    def chv(self, eng):
        return self.val(self.sem("ch_" + eng))

    def run_block(self):
        nc = self.nc
        with nc.Block() as block:
            @block.sync
            def _(e):
                for t in self.th["sync"]:
                    t(e)

            @block.vector
            def _(e):
                for t in self.th["vector"]:
                    t(e)

            @block.scalar
            def _(e):
                for t in self.th["scalar"]:
                    t(e)

            @block.tensor
            def _(e):
                for t in self.th["tensor"]:
                    t(e)

            @block.gpsimd
            def _(e):
                for t in self.th["gpsimd"]:
                    t(e)


def mk_nc():
    return bass.Bass("TRN2", target_bir_lowering=False, debug=False)


def run_launch(build_fn, in_maps, trace=False):
    nc, out_names = build_fn()
    res = run_bass_kernel_spmd(nc, in_maps, core_ids=list(range(len(in_maps))),
                               trace=trace)
    return res



def emit_ln(p, xs_aps, xn_f32_ap, xsum_ap, pmax_dram_ap, tag, xs_dts=None,
            xn_bf16_ap=None):
    nin = len(xs_aps)
    dts = xs_dts if xs_dts is not None else [F32] * nin
    xt = p.sb(tag + "xt", [128, D], F32)
    xnb = p.sb(tag + "xnb", [128, D], BF16) if xn_bf16_ap is not None else None
    bv_prev = 0
    xe = {}
    for dt in set(dts[1:]):
        xe[dt] = p.sb(tag + "xe%d" % len(xe), [128, D], dt)
    sq = p.sb(tag + "sq", [128, D], F32)
    scol = p.sb(tag + "scol", [128, 1], F32)
    mcol = p.sb(tag + "mcol", [128, 1], F32)
    msq = p.sb(tag + "msq", [128, 1], F32)
    vcol = p.sb(tag + "vcol", [128, 1], F32)
    rstd = p.sb(tag + "rstd", [128, 1], F32)
    xnf = p.sb(tag + "xnf", [128, D], F32)
    maxacc = p.sb(tag + "maxacc", [128, 1], F32)
    maxtmp = p.sb(tag + "maxtmp", [128, 1], F32)

    v_done = 0
    for it in range(T // 128):
        t0 = it * 128
        so_prev = p.val(p.sem(tag + "st"))
        if it > 0:
            p.wait("sync", "ch_vector", v_done)
        ldv = p.dma("sync", xt[:, :], xs_aps[0][t0:t0 + 128, :])
        p.wait("vector", "dma_sync", ldv)
        av = 0
        if nin > 1:
            for j in range(1, nin):
                xej = xe[dts[j]]
                ldv = p.dma("sync", xej[:, :], xs_aps[j][t0:t0 + 128, :])
                p.wait("vector", "dma_sync", ldv)
                av = p.op("vector", lambda e, xej=xej: e.tensor_add(
                    xt[:, :], xt[:, :], xej[:, :]))
                if j < nin - 1:
                    p.wait("sync", "ch_vector", av)
            if xsum_ap is not None:
                p.wait("sync", "ch_vector", av)
                p.dma("sync", xsum_ap[t0:t0 + 128, :], xt[:, :], sem=tag + "st")

        p.op("vector", lambda e: e.reduce_sum(scol[:, :], xt[:, :], AXX))
        mv = p.op("vector", lambda e: e.tensor_scalar_mul(mcol[:, :], scol[:, :], 1.0 / D))
        p.wait("scalar", "dma_sync", ldv)
        if av:
            p.wait("scalar", "ch_vector", av)
        if it > 0:
            p.wait("scalar", "ch_vector", v_done)
        sv = p.op("scalar", lambda e: e.activation(sq[:, :], xt[:, :], AF.Square))
        p.wait("vector", "ch_scalar", sv)
        p.op("vector", lambda e: e.reduce_sum(vcol[:, :], sq[:, :], AXX))
        p.op("vector", lambda e: e.tensor_mul(msq[:, :], mcol[:, :], mcol[:, :]))
        p.op("vector", lambda e: e.tensor_scalar_mul(vcol[:, :], vcol[:, :], 1.0 / D))
        p.op("vector", lambda e: e.tensor_sub(vcol[:, :], vcol[:, :], msq[:, :]))
        ev = p.op("vector", lambda e: e.tensor_scalar_add(vcol[:, :], vcol[:, :], float(EPS)))
        p.wait("scalar", "ch_vector", ev)
        qv = p.op("scalar", lambda e: e.activation(rstd[:, :], vcol[:, :], AF.Sqrt))
        p.wait("vector", "ch_scalar", qv)
        p.op("vector", lambda e: e.reciprocal(rstd[:, :], rstd[:, :]))
        p.wait("vector", tag + "st", so_prev)
        if xn_bf16_ap is not None and bv_prev:
            p.wait("vector", "ch_scalar", p.chv("scalar"))
        nv = p.op("vector", lambda e: e.tensor_scalar(
            xnf[:, :], xt[:, :], mcol[:, 0:1], rstd[:, 0:1],
            ALU.subtract, ALU.mult))
        if pmax_dram_ap is not None:
            p.op("vector", lambda e: e.reduce_max(maxtmp[:, :], xnf[:, :], AXX,
                                                  apply_absolute_value=True))
            if it == 0:
                nv = p.op("vector", lambda e: e.tensor_copy(maxacc[:, :], maxtmp[:, :]))
            else:
                nv = p.op("vector", lambda e: e.tensor_max(maxacc[:, :], maxacc[:, :],
                                                           maxtmp[:, :]))
        v_done = nv
        p.wait("sync", "ch_vector", nv)
        p.dma("sync", xn_f32_ap[t0:t0 + 128, :], xnf[:, :], sem=tag + "st")
        if xn_bf16_ap is not None:
            p.wait("scalar", "ch_vector", nv)
            if bv_prev:
                p.wait("scalar", tag + "st", bv_prev)
            bcv = p.op("scalar", lambda e: e.copy(xnb[:, :], xnf[:, :]))
            v_done = max(v_done, 0)
            p.wait("sync", "ch_scalar", bcv)
            bv_prev = p.dma("sync", xn_bf16_ap[t0:t0 + 128, :], xnb[:, :],
                            sem=tag + "st")

    if pmax_dram_ap is not None:
        p.dma("sync", pmax_dram_ap, maxacc[:, :], sem=tag + "st")
    p.wait("sync", "dma_sync", p.val("dma_sync"))
    p.wait("sync", tag + "st", p.val(tag + "st"))


def emit_transpose(p, src_ap, dst_ap, R, C, ident_sb, tag, src_dt=F32,
                   dst_dt=BF16):
    NS = 4
    tin = [p.sb(tag + "tin%d" % b, [128, C], src_dt) for b in range(2)]
    if src_dt != dst_dt:
        tb = [p.sb(tag + "tb%d" % b, [128, C], dst_dt) for b in range(2)]
    else:
        tb = tin
    tout = p.sb(tag + "tout", [128, NS * 128], dst_dt)
    pst = [p.ps(tag + "pst%d" % b, [128, 128], dst_dt) for b in range(2)]

    ev = {0: 0, 1: 0}
    stv = {sl: 0 for sl in range(NS)}
    cast_v = {0: 0, 1: 0}
    ldv = {}
    nblk = R // 128
    ldv[0] = p.dma("sync", tin[0][:, :], src_ap[0:128, :])
    if nblk > 1:
        ldv[1] = p.dma("sync", tin[1][:, :], src_ap[128:256, :])
    sl = 0
    for it in range(nblk):
        ib = it % 2
        if src_dt != dst_dt:
            p.wait("scalar", "dma_sync", ldv[it])
            if cast_v[ib]:
                p.wait("scalar", tag + "t", (it - 1) * (C // 128))
            cast_v[ib] = p.op("scalar", lambda e, ib=ib: e.copy(
                tb[ib][:, :], tin[ib][:, :]))
            p.wait("tensor", "ch_scalar", cast_v[ib])
        else:
            p.wait("tensor", "dma_sync", ldv[it])
        for ck in range(C // 128):
            pb = (it * (C // 128) + ck) % 2
            if ev[pb]:
                p.wait("tensor", "ch_vector", ev[pb])
            tv = p.op("tensor", lambda e, ck=ck, ib=ib, pb=pb: e.transpose(
                pst[pb][:, :], tb[ib][:, ck * 128:(ck + 1) * 128], ident_sb[:, :]),
                extra_sem=tag + "t")
            p.wait("vector", tag + "t", tv)
            if stv[sl]:
                p.wait("vector", tag + "st", stv[sl])
            ev[pb] = p.op("vector", lambda e, sl=sl, pb=pb: e.tensor_copy(
                tout[:, sl * 128:(sl + 1) * 128], pst[pb][:, :]))
            p.wait("sync", "ch_vector", ev[pb])
            stv[sl] = p.dma("sync", dst_ap[ck * 128:ck * 128 + 128,
                                           it * 128:it * 128 + 128],
                            tout[:, sl * 128:(sl + 1) * 128], sem=tag + "st")
            sl = (sl + 1) % NS
        if it + 2 < nblk:
            if src_dt != dst_dt:
                p.wait("sync", "ch_scalar", cast_v[ib])
            else:
                p.wait("sync", tag + "t", (it + 1) * (C // 128))
            ldv[it + 2] = p.dma("sync", tin[ib][:, :],
                                src_ap[(it + 2) * 128:(it + 3) * 128, :])
    p.wait("sync", "dma_sync", p.val("dma_sync"))
    p.wait("sync", tag + "st", p.val(tag + "st"))



def build_ln0():
    nc = mk_nc()
    X = nc.dram_tensor("X", [T, D], F32, kind="ExternalInput").ap()
    xn = nc.dram_tensor("xn_f32", [T, D], F32, kind="ExternalOutput").ap()
    xn_tm = nc.dram_tensor("xn_tm", [T, D], BF16, kind="ExternalOutput").ap()
    pmax = nc.dram_tensor("pmax", [128, 1], F32, kind="ExternalOutput").ap()
    with ExitStack() as st:
        p = Plan(nc, st)
        with p.scope():
            emit_ln(p, [X], xn, None, pmax, "ln0", xn_bf16_ap=xn_tm)
        p.run_block()
    return nc, ["xn_f32", "xn_tm", "pmax"]


def emit_scalar_combine(p, pmax_all_ap, clip_ap, out_scale_ap, out_recip_sb, tag,
                        extra_mul=1.0):
    nc = p.nc
    pm = p.sb(tag + "pm", [8, 128], F32)
    red = p.sb(tag + "red", [8, 1], F32)
    row = p.sb(tag + "row", [1, 8], F32)
    g1 = p.sb(tag + "g1", [1, 1], F32)
    cl = p.sb(tag + "cl", [1, 1], F32)
    sc = p.sb(tag + "sc", [1, 1], F32)
    rc = p.sb(tag + "rc", [1, 1], F32)
    scr = nc.dram_tensor(tag + "scr", [8, 1], F32).ap()
    scr2 = nc.dram_tensor(tag + "scr2", [1, 1], F32).ap()

    v = p.dma("sync", pm[:, :], pmax_all_ap)
    v2 = p.dma("sync", cl[:, :], clip_ap)
    p.wait("vector", "dma_sync", v2)
    rv = p.op("vector", lambda e: e.reduce_max(red[:, :], pm[:, :], AXX))
    p.wait("sync", "ch_vector", rv)
    sv = p.dma("sync", scr, red[:, :])
    p.wait("sync", "dma_sync", sv)
    lv = p.dma("sync", row[:, :], scr.rearrange("a 1 -> 1 a"))
    p.wait("vector", "dma_sync", lv)
    p.op("vector", lambda e: e.reduce_max(g1[:, :], row[:, :], AXX))
    p.wait("scalar", "dma_sync", v2)
    gv = p.op("scalar", lambda e: e.activation(cl[:, :], cl[:, :], AF.Sigmoid))
    p.wait("vector", "ch_scalar", gv)
    p.op("vector", lambda e: e.tensor_mul(sc[:, :], g1[:, :], cl[:, :]))
    scv = p.op("vector", lambda e: e.tensor_scalar_mul(sc[:, :], sc[:, :],
                                                       float(extra_mul) / QMAX))
    p.wait("sync", "ch_vector", scv)
    p.dma("sync", out_scale_ap, sc[:, :])
    rcv = p.op("vector", lambda e: e.reciprocal(rc[:, :], sc[:, :]))
    p.wait("sync", "ch_vector", rcv)
    bv = p.dma("sync", scr2, rc[:, :])
    p.wait("sync", "dma_sync", bv)
    bb = p.dma("sync", out_recip_sb[:, :], scr2.broadcast_to([128, 1]))
    return bb


def emit_quant(p, src_f32_ap, dst_bf16_ap, R, C, recip_sb, recip_ready, tag,
               do_clip=True):
    xt = [p.sb(tag + "xt%d" % b, [128, C], F32) for b in range(2)]
    qf = [p.sb(tag + "qf%d" % b, [128, C], F32) for b in range(2)]
    qi = p.sb(tag + "qi", [128, C], I32)
    qb = [p.sb(tag + "qb%d" % b, [128, C], BF16) for b in range(2)]
    p.wait("scalar", "dma_sync", recip_ready)
    nblk = R // 128
    ldv = {}
    ldv[0] = p.dma("sync", xt[0][:, :], src_f32_ap[0:128, :])
    if nblk > 1:
        ldv[1] = p.dma("sync", xt[1][:, :], src_f32_ap[128:256, :])
    sc_v = {0: 0, 1: 0}
    dv_v = {0: 0, 1: 0}
    st_v = {0: 0, 1: 0}
    for it in range(nblk):
        b = it % 2
        p.wait("scalar", "dma_sync", ldv[it])
        if dv_v[b]:
            p.wait("scalar", "ch_vector", dv_v[b])
        sc_v[b] = p.op("scalar", lambda e, b=b: e.activation(
            qf[b][:, :], xt[b][:, :], AF.Copy, scale=recip_sb[:, 0:1]))
        if it + 2 < nblk:
            p.wait("sync", "ch_scalar", sc_v[b])
            ldv[it + 2] = p.dma("sync", xt[b][:, :],
                                src_f32_ap[(it + 2) * 128:(it + 3) * 128, :])
        p.wait("vector", "ch_scalar", sc_v[b])
        p.op("vector", lambda e, b=b: e.tensor_copy(qi[:, :], qf[b][:, :]))
        if do_clip:
            p.op("vector", lambda e: e.tensor_scalar(qi[:, :], qi[:, :], 127, -128,
                                                     ALU.min, ALU.max))
        if st_v[b]:
            p.wait("vector", tag + "st", st_v[b])
        dv_v[b] = p.op("vector", lambda e, b=b: e.tensor_copy(qb[b][:, :], qi[:, :]))
        p.wait("sync", "ch_vector", dv_v[b])
        st_v[b] = p.dma("sync", dst_bf16_ap[it * 128:(it + 1) * 128, :],
                        qb[b][:, :], sem=tag + "st")
    p.wait("sync", "dma_sync", p.val("dma_sync"))
    p.wait("sync", tag + "st", p.val(tag + "st"))


def emit_cast(p, src_f32_ap, dst_bf16_ap, R, C, tag):
    xt = p.sb(tag + "xt", [128, C], F32)
    xb = p.sb(tag + "xb", [128, C], BF16)
    last = 0
    for it in range(R // 128):
        r0 = it * 128
        if it > 0:
            p.wait("sync", "ch_scalar", last)
        ldv = p.dma("sync", xt[:, :], src_f32_ap[r0:r0 + 128, :])
        p.wait("scalar", "dma_sync", ldv)
        if it > 0:
            p.wait("scalar", tag + "st", so_prev)
        last = p.op("scalar", lambda e: e.copy(xb[:, :], xt[:, :]))
        p.wait("sync", "ch_scalar", last)
        so_prev = p.dma("sync", dst_bf16_ap[r0:r0 + 128, :], xb[:, :], sem=tag + "st")
    p.wait("sync", "dma_sync", p.val("dma_sync"))
    p.wait("sync", tag + "st", p.val(tag + "st"))


def build_qt():
    nc = mk_nc()
    xn = nc.dram_tensor("xn_f32", [T, D], F32, kind="ExternalInput").ap()
    xn_tm = nc.dram_tensor("xn_tm", [T, D], BF16, kind="ExternalInput").ap()
    pmax_all = nc.dram_tensor("pmax_all", [8, 128], F32, kind="ExternalInput").ap()
    clip_x = nc.dram_tensor("clip_x", [1, 1], F32, kind="ExternalInput").ap()
    identc = nc.dram_tensor("identc", [128, 128], BF16, kind="ExternalInput").ap()
    xn_fm = nc.dram_tensor("xn_fm", [D, T], BF16, kind="ExternalOutput").ap()
    xq_fm = nc.dram_tensor("xq_fm", [D, T], BF16, kind="ExternalOutput").ap()
    scale_x = nc.dram_tensor("scale_x", [1, 1], F32, kind="ExternalOutput").ap()
    xq_tm = nc.dram_tensor("xq_tm", [T, D], BF16).ap()

    with ExitStack() as st:
        p = Plan(nc, st)
        ident = p.sb("ident", [128, 128], BF16)
        iv = p.dma("sync", ident[:, :], identc)
        recip = p.sb("recip", [128, 1], F32)
        with p.scope():
            rv = emit_scalar_combine(p, pmax_all, clip_x, scale_x, recip, "sc")
        with p.scope():
            emit_quant(p, xn, xq_tm, T, D, recip, p.val("dma_sync"), "q")
        p.wait("tensor", "dma_sync", iv)
        with p.scope():
            emit_transpose(p, xq_tm, xq_fm, T, D, ident, "t1", src_dt=BF16)
        with p.scope():
            emit_transpose(p, xn_tm, xn_fm, T, D, ident, "t2", src_dt=BF16)
        p.run_block()
    return nc, ["xn_fm", "xq_fm", "scale_x"]


def emit_wquant(p, w_ap, wq_tm_ap, wscale_ap, R, C, tag):
    wt = p.sb(tag + "wt", [128, C], F32)
    wmax = p.sb(tag + "wmax", [128, 1], F32)
    wrec = p.sb(tag + "wrec", [128, 1], F32)
    qf = p.sb(tag + "qf", [128, C], F32)
    qi = p.sb(tag + "qi", [128, C], I32)
    qb = p.sb(tag + "qb", [128, C], BF16)
    last = 0
    for it in range(R // 128):
        r0 = it * 128
        if it > 0:
            p.wait("sync", "ch_vector", last)
        ldv = p.dma("sync", wt[:, :], w_ap[r0:r0 + 128, :])
        p.wait("vector", "dma_sync", ldv)
        if it > 0:
            p.wait("vector", tag + "st", so_prev)
        p.op("vector", lambda e: e.reduce_max(wmax[:, :], wt[:, :], AXX,
                                              apply_absolute_value=True))
        sv = p.op("vector", lambda e: e.tensor_scalar_mul(wmax[:, :], wmax[:, :],
                                                          1.0 / QMAX))
        p.wait("sync", "ch_vector", sv)
        p.dma("sync", wscale_ap[r0:r0 + 128, :], wmax[:, :], sem=tag + "st")
        p.op("vector", lambda e: e.reciprocal(wrec[:, :], wmax[:, :]))
        p.op("vector", lambda e: e.tensor_scalar_mul(qf[:, :], wt[:, :], wrec[:, 0:1]))
        p.op("vector", lambda e: e.tensor_copy(qi[:, :], qf[:, :]))
        last = p.op("vector", lambda e: e.tensor_copy(qb[:, :], qi[:, :]))
        p.wait("sync", "ch_vector", last)
        so_prev = p.dma("sync", wq_tm_ap[r0:r0 + 128, :], qb[:, :], sem=tag + "st")
    p.wait("sync", "dma_sync", p.val("dma_sync"))
    p.wait("sync", tag + "st", p.val(tag + "st"))


def build_ffn0a():
    nc = mk_nc()
    xq_fm = nc.dram_tensor("xq_fm", [D, T], BF16, kind="ExternalInput").ap()
    scale_x = nc.dram_tensor("scale_x", [1, 1], F32, kind="ExternalInput").ap()
    w_gate = nc.dram_tensor("w_gate", [DI, D], F32, kind="ExternalInput").ap()
    w_up = nc.dram_tensor("w_up", [DI, D], F32, kind="ExternalInput").ap()
    identc = nc.dram_tensor("identc", [128, 128], BF16, kind="ExternalInput").ap()
    h_out = nc.dram_tensor("h_f32", [DI, T], F32, kind="ExternalOutput").ap()
    pmax_h = nc.dram_tensor("pmax_h", [128, 1], F32, kind="ExternalOutput").ap()

    with ExitStack() as st:
        p = Plan(nc, st)
        ident = p.sb("ident", [128, 128], BF16)
        iv = p.dma("sync", ident[:, :], identc)
        emit_ffn0a(p, xq_fm, scale_x, w_gate, w_up, ident, h_out, pmax_h)
        p.wait("sync", "dma_sync", p.val("dma_sync"))
        p.wait("sync", "fa_st", p.val("fa_st"))
        p.run_block()
    return nc, ["h_f32", "pmax_h"]


def emit_ffn0a(p, xq_fm, scale_x, w_gate, w_up, ident, h_out, pmax_h):
    nc = p.nc
    wgq_tm = nc.dram_tensor("wgq_tm", [DI, D], BF16).ap()
    wuq_tm = nc.dram_tensor("wuq_tm", [DI, D], BF16).ap()
    wgq_fm = nc.dram_tensor("wgq_fm", [D, DI], BF16).ap()
    wuq_fm = nc.dram_tensor("wuq_fm", [D, DI], BF16).ap()
    wgs = nc.dram_tensor("wgs", [DI, 1], F32).ap()
    wus = nc.dram_tensor("wus", [DI, 1], F32).ap()

    TC = 512
    with p.scope():
        with p.scope():
            emit_wquant(p, w_gate, wgq_tm, wgs, DI, D, "wg")
        with p.scope():
            emit_wquant(p, w_up, wuq_tm, wus, DI, D, "wu")
        with p.scope():
            emit_transpose(p, wgq_tm, wgq_fm, DI, D, ident, "tg", src_dt=BF16)
        with p.scope():
            emit_transpose(p, wuq_tm, wuq_fm, DI, D, ident, "tu", src_dt=BF16)

        scx = p.sb("scx", [128, 1], F32)
        sv = p.dma("sync", scx[:, :], scale_x.broadcast_to([128, 1]))

        wg_sb = p.sb("wg_sb", [128, 8 * DI], BF16)
        wu_sb = p.sb("wu_sb", [128, 8 * DI], BF16)
        xq_sb = p.sb("xq_sb", [128, 8 * TC], BF16)
        wv = 0
        for k in range(8):
            wv = p.dma("sync", wg_sb[:, k * DI:(k + 1) * DI], wgq_fm[k * 128:(k + 1) * 128, :])
            wv = p.dma("sync", wu_sb[:, k * DI:(k + 1) * DI], wuq_fm[k * 128:(k + 1) * 128, :])
        gsc = p.sb("gsc", [128, 22], F32)
        usc = p.sb("usc", [128, 22], F32)
        p.fence()
        g1 = p.dma("sync", gsc[:, :], wgs.rearrange("(a p) 1 -> p a", p=128), allow_slow_non_contiguous=True)
        g2 = p.dma("sync", usc[:, :], wus.rearrange("(a p) 1 -> p a", p=128), allow_slow_non_contiguous=True)
        p.fence()
        p.wait("vector", "dma_sync", max(sv, g1, g2))
        p.op("vector", lambda e: e.tensor_scalar_mul(gsc[:, :], gsc[:, :], scx[:, 0:1]))
        cv = p.op("vector", lambda e: e.tensor_scalar_mul(usc[:, :], usc[:, :], scx[:, 0:1]))
        p.wait("scalar", "ch_vector", cv)

        pg = [p.ps("pg%d" % b, [128, TC]) for b in range(2)]
        pu = [p.ps("pu%d" % b, [128, TC]) for b in range(2)]
        sg = [p.sb("sg%d" % b, [128, TC], F32) for b in range(2)]
        su = [p.sb("su%d" % b, [128, TC], F32) for b in range(2)]
        ht = [p.sb("ht%d" % b, [128, TC], F32) for b in range(2)]
        maxacc = p.sb("maxacc", [128, 1], F32)
        maxtmp = p.sb("maxtmp", [128, 1], F32)
        zv = p.op("vector", lambda e: e.memset(maxacc[:, :], 0.0))

        ev_u = {0: 0, 1: 0}
        hv = {0: 0, 1: 0}
        st_prev = {0: 0, 1: 0}
        p.sem("mmend")
        nchunk = 0
        for tc in range(T // TC):
            t0 = tc * TC
            if tc > 0:
                p.wait("sync", "mmend", 22 * tc)
            for k in range(8):
                xv = p.dma("sync", xq_sb[:, k * TC:(k + 1) * TC],
                           xq_fm[k * 128:(k + 1) * 128, t0:t0 + TC])
            p.wait("tensor", "dma_sync", xv)
            for ic in range(22):
                i0 = ic * 128
                b = nchunk % 2
                if ev_u[b]:
                    p.wait("tensor", "ch_scalar", ev_u[b])
                for k in range(8):
                    p.op("tensor", lambda e, k=k, i0=i0, b=b: e.matmul(
                        pg[b][:, :], wg_sb[:, k * DI + i0:k * DI + i0 + 128],
                        xq_sb[:, k * TC:(k + 1) * TC],
                        start=(k == 0), stop=(k == 7)))
                for k in range(8):
                    p.op("tensor", lambda e, k=k, i0=i0, b=b: e.matmul(
                        pu[b][:, :], wu_sb[:, k * DI + i0:k * DI + i0 + 128],
                        xq_sb[:, k * TC:(k + 1) * TC],
                        start=(k == 0), stop=(k == 7)), extra_sem="mmend" if k == 7 else None)
                mv = p.val("mmend")
                p.wait("scalar", "mmend", mv)
                if hv[b]:
                    p.wait("scalar", "ch_vector", hv[b])
                p.op("scalar", lambda e, ic=ic, b=b: e.activation(
                    sg[b][:, :], pg[b][:, :], AF.Silu, scale=gsc[:, ic:ic + 1]))
                ev_u[b] = p.op("scalar", lambda e, ic=ic, b=b: e.activation(
                    su[b][:, :], pu[b][:, :], AF.Copy, scale=usc[:, ic:ic + 1]))
                p.wait("vector", "ch_scalar", ev_u[b])
                if st_prev[b]:
                    p.wait("vector", "fa_st", st_prev[b])
                hv[b] = p.op("vector", lambda e, b=b: e.tensor_mul(
                    ht[b][:, :], sg[b][:, :], su[b][:, :]))
                p.op("vector", lambda e, b=b: e.reduce_max(
                    maxtmp[:, :], ht[b][:, :], AXX, apply_absolute_value=True))
                hv[b] = p.op("vector", lambda e: e.tensor_max(
                    maxacc[:, :], maxacc[:, :], maxtmp[:, :]))
                p.wait("sync", "ch_vector", hv[b])
                st_prev[b] = p.dma("sync", h_out[i0:i0 + 128, t0:t0 + TC],
                                   ht[b][:, :], sem="fa_st")
                nchunk += 1
        lv = p.op("vector", lambda e: e.tensor_copy(maxtmp[:, :], maxacc[:, :]))
        p.wait("sync", "ch_vector", lv)
        p.dma("sync", pmax_h, maxacc[:, :], sem="fa_st")
        p.wait("sync", "fa_st", p.val("fa_st"))


def build_ffn0b():
    nc = mk_nc()
    h_f32 = nc.dram_tensor("h_f32", [DI, T], F32, kind="ExternalInput").ap()
    pmax_all = nc.dram_tensor("pmax_h_all", [8, 128], F32, kind="ExternalInput").ap()
    clip_down = nc.dram_tensor("clip_down", [1, 1], F32, kind="ExternalInput").ap()
    w_down = nc.dram_tensor("w_down", [D, DI], F32, kind="ExternalInput").ap()
    identc = nc.dram_tensor("identc", [128, 128], BF16, kind="ExternalInput").ap()
    ffn0 = nc.dram_tensor("ffn0", [T, D], F32, kind="ExternalOutput").ap()

    with ExitStack() as st:
        p = Plan(nc, st)
        ident = p.sb("ident", [128, 128], BF16)
        iv = p.dma("sync", ident[:, :], identc)
        emit_ffn0b(p, h_f32, pmax_all, clip_down, w_down, ident, ffn0)
        p.wait("sync", "dma_sync", p.val("dma_sync"))
        p.wait("sync", "fb_st", p.val("fb_st"))
        p.run_block()
    return nc, ["ffn0"]


def emit_ffn0b(p, h_f32, pmax_all, clip_down, w_down, ident, ffn0):
    nc = p.nc
    hq_tm = nc.dram_tensor("hq_tm", [DI, T], BF16).ap()
    wdq_tm = nc.dram_tensor("wdq_tm", [D, DI], BF16).ap()
    wdq_fm = nc.dram_tensor("wdq_fm", [DI, D], BF16).ap()
    wds = nc.dram_tensor("wds", [D, 1], F32).ap()
    scale_h_d = nc.dram_tensor("scale_h_d", [1, 1], F32).ap()
    rowsc_d = nc.dram_tensor("rowsc_d", [1, D], F32).ap()

    with p.scope():
        recip = p.sb("recipB", [128, 1], F32)
        with p.scope():
            rv = emit_scalar_combine(p, pmax_all, clip_down, scale_h_d, recip, "sc")
        with p.scope():
            emit_quant(p, h_f32, hq_tm, DI, T, recip, p.val("dma_sync"), "q")
        with p.scope():
            emit_wquant(p, w_down, wdq_tm, wds, D, DI, "wd")
        with p.scope():
            emit_transpose(p, wdq_tm, wdq_fm, D, DI, ident, "td", src_dt=BF16)

        shc = p.sb("shc", [128, 1], F32)
        s1 = p.dma("sync", shc[:, :], scale_h_d.broadcast_to([128, 1]))
        wsc = p.sb("wsc", [128, 8], F32)
        s2 = p.dma("sync", wsc[:, :], wds.rearrange("(a p) 1 -> p a", p=128), allow_slow_non_contiguous=True)
        p.wait("vector", "dma_sync", max(s1, s2))
        sv = p.op("vector", lambda e: e.tensor_scalar_mul(wsc[:, :], wsc[:, :], shc[:, 0:1]))
        p.wait("sync", "ch_vector", sv)
        s3 = p.dma("sync", rowsc_d.rearrange("o (a p) -> (o p) a", p=128), wsc[:, :], allow_slow_non_contiguous=True)
        p.wait("sync", "dma_sync", s3)
        rowsc = p.sb("rowsc", [128, D], F32)
        s4 = p.dma("sync", rowsc[:, :], rowsc_d.broadcast_to([128, D]))
        p.fence()

        hq_sb = p.sb("hq_sb", [128, 22 * T], BF16)
        for k in range(22):
            hv = p.dma("sync", hq_sb[:, k * T:(k + 1) * T],
                       hq_tm[k * 128:(k + 1) * 128, :])
        p.fence()
        p.wait("tensor", "dma_sync", p.val("dma_sync"))
        wd_sl = [p.sb("wd_sl%d" % b, [128, 512], BF16) for b in range(2)]
        pd = [p.ps("pd%d" % b, [128, 512]) for b in range(4)]
        ot = p.sb("ot", [128, 512], F32)

        NG = 4
        DEBUG_SERIAL = False
        ev = 0
        stv = 0
        wready = {}
        p.wait("vector", "dma_sync", s4)
        p.sem("wfree")
        p.sem("dwd0")
        p.sem("dwd1")
        for grp in range(32 // NG):
            for hh in range(2):
                for k in range(22):
                    wb = k % 2
                    tgt = p.val("wfree") - 1
                    if tgt > 0:
                        p.wait("sync", "wfree", tgt)
                    wv = p.dma("sync", wd_sl[wb][:, :],
                               wdq_fm[k * 128:(k + 1) * 128, hh * 512:hh * 512 + 512],
                               sem="dwd%d" % wb)
                    p.wait("tensor", "dwd%d" % wb, wv)
                    if k == 0 and ev:
                        p.wait("tensor", "ch_vector", ev)
                    for j in range(NG):
                        n2 = grp * NG + j
                        p.op("tensor", lambda e, k=k, n2=n2, j=j, wb=wb: e.matmul(
                            pd[j][:, :],
                            hq_sb[:, k * T + n2:k * T + n2 + 4065:32],
                            wd_sl[wb][:, :],
                            start=(k == 0), stop=(k == 21)),
                            extra_sem="wfree" if j == NG - 1 else None)
                mv = p.val("wfree")
                p.wait("vector", "wfree", mv)
                for j in range(NG):
                    n2 = grp * NG + j
                    if stv:
                        p.wait("vector", "fb_st", stv)
                    ev = p.op("vector", lambda e, j=j, hh=hh: e.tensor_mul(
                        ot[:, :], pd[j][:, :], rowsc[:, hh * 512:hh * 512 + 512]))
                    p.wait("sync", "ch_vector", ev)
                    stv = p.dma("sync", ffn0[n2:T:32, hh * 512:hh * 512 + 512], ot[:, :],
                                sem="fb_st")
                if DEBUG_SERIAL:
                    p.wait("tensor", "dma_sync", stv)
        p.wait("sync", "fb_st", p.val("fb_st"))


def _l1_internal_drams(nc):
    kind = "ExternalOutput" if DEBUG_L1 else "Internal"
    h_dbg = nc.dram_tensor("h_dbg", [D, T], BF16, kind=kind).ap()
    wr_dbg = nc.dram_tensor("wr_dbg", [512, 512], BF16, kind=kind).ap()
    wi_dbg = nc.dram_tensor("wi_dbg", [512, 512], BF16, kind=kind).ap()
    attn1_fm = nc.dram_tensor("attn1_fm", [D, T], BF16, kind=kind).ap()
    attn1_tm = nc.dram_tensor("attn1_tm", [T, D], BF16, kind=kind).ap()
    return (h_dbg, wr_dbg, wi_dbg, attn1_fm, attn1_tm)


def emit_l1(p, xn1_fm, x1, Wz, bz, W_g, b_g, G1R, G2R, G1I, G2I,
            b_re, b_im, onesc, ones512c, repac, perioc, identc, identf,
            OUT, dbg):
    nc = p.nc
    h_dbg, wr_dbg, wi_dbg, attn1_fm, attn1_tm = dbg
    ident = p.sb("ident", [128, 128], BF16)
    iv = p.dma("sync", ident[:, :], identc)
    identF = p.sb("identF", [128, 128], F32)
    p.dma("sync", identF[:, :], identf)
    ones = p.sb("ones", [1, 128], BF16)
    p.dma("sync", ones[:, :], onesc)
    ones512 = p.sb("ones512", [1, 512], BF16)
    p.dma("sync", ones512[:, :], ones512c)
    xn1_sb = p.sb("xn1_sb", [128, 8 * T], BF16)
    for k in range(8):
        xv = p.dma("sync", xn1_sb[:, k * T:(k + 1) * T],
                   xn1_fm[k * 128:(k + 1) * 128, :])
    p.fence()
    p.wait("tensor", "dma_sync", p.val("dma_sync"))

    with p.scope() as mid:
        h_sb = self_h = p.scopes[-1].enter_context(
            nc.sbuf_tensor("h_sb", [128, 8 * T], BF16))

        def _phaseA():
            wsl = p.sb("wslA", [128, 8 * 128], BF16)
            wtmp = p.sb("wtmpA", [128, 8 * 128], F32)
            browf = p.sb("browfA", [1, 128], F32)
            brow = p.sb("browA", [1, 128], BF16)
            zrow = p.sb("zrowA", [128, T], F32)
            hrow = p.sb("hrowA", [128, T], F32)
            pp = [p.ps("ppA%d" % b, [128, 512]) for b in range(2)]
            for fc in range(8):
                fam = "f" if fc < 4 else "s"
                sub = fc % 4
                for which in ("z", "h"):
                    wap = Wz[which + fam]
                    for k in range(8):
                        lv = p.dma("sync", wtmp[:, k * 128:(k + 1) * 128],
                                   wap[k * 128:(k + 1) * 128,
                                       sub * 128:sub * 128 + 128])
                    p.wait("scalar", "dma_sync", lv)
                    cv = p.op("scalar", lambda e: e.copy(wsl[:, :], wtmp[:, :]))
                    p.wait("sync", "ch_scalar", cv)
                    bv = p.dma("sync", browf[:, :],
                               bz[which + fam].rearrange("a o -> o a")[
                                   :, sub * 128:sub * 128 + 128])
                    p.wait("scalar", "dma_sync", bv)
                    cb = p.op("scalar", lambda e: e.copy(brow[:, :], browf[:, :]))
                    p.wait("tensor", "ch_scalar", cb)
                    for tc2 in range(8):
                        b = tc2 % 2
                        for k in range(8):
                            p.op("tensor", lambda e, k=k, tc2=tc2, b=b: e.matmul(
                                pp[b][:, :], wsl[:, k * 128:(k + 1) * 128],
                                xn1_sb[:, k * T + tc2 * 512:k * T + tc2 * 512 + 512],
                                start=(k == 0), stop=False))
                        p.op("tensor", lambda e, b=b: e.matmul(
                            pp[b][:, :], brow[:, :], ones512[:, :],
                            start=False, stop=True), extra_sem="l1mm")
                        mv = p.val("l1mm")
                        p.wait("scalar", "l1mm", mv)
                        if which == "z":
                            ev = p.op("scalar", lambda e, tc2=tc2, b=b: e.activation(
                                zrow[:, tc2 * 512:tc2 * 512 + 512], pp[b][:, :],
                                AF.Sigmoid))
                        else:
                            ev = p.op("scalar", lambda e, tc2=tc2, b=b: e.copy(
                                hrow[:, tc2 * 512:tc2 * 512 + 512], pp[b][:, :]))
                        p.wait("tensor", "ch_scalar", ev)
                p.wait("vector", "ch_scalar", ev)
                p.op("vector", lambda e: e.tensor_mul(hrow[:, :], zrow[:, :],
                                                      hrow[:, :]))
                p.op("vector", lambda e: e.tensor_scalar(
                    zrow[:, :], zrow[:, :], -1.0, 1.0, ALU.mult, ALU.add))
                p.op("vector", lambda e: e.tensor_tensor_scan(
                    zrow[:, :], zrow[:, :], hrow[:, :], 0.0, ALU.mult, ALU.add))
                sv = p.op("vector", lambda e, fc=fc: e.tensor_copy(
                    h_sb[:, fc * T:(fc + 1) * T], zrow[:, :]))
                p.wait("scalar", "ch_vector", sv)
                p.wait("tensor", "ch_vector", sv)
                if DEBUG_L1:
                    p.wait("sync", "ch_vector", sv)
                    p.dma("sync", h_dbg[fc * 128:(fc + 1) * 128, :],
                          h_sb[:, fc * T:(fc + 1) * T])

        with p.scope():
            _phaseA()

        def _phaseB():
            wgl = p.sb("wglB", [128, 8 * 128], BF16)
            wtmp = p.sb("wtmpB", [128, 8 * 128], F32)
            browf = p.sb("browfB", [1, 128], F32)
            brow = p.sb("browB", [1, 128], BF16)
            gt = p.sb("gtB", [128, 512], F32)
            at_f = p.sb("at_fB", [128, 512], F32)
            at_b = p.sb("at_bB", [128, 512], BF16)
            pp = [p.ps("ppB%d" % b, [128, 512]) for b in range(2)]
            stv = 0
            gtv = 0
            for dc in range(8):
                for k in range(8):
                    lv = p.dma("sync", wtmp[:, k * 128:(k + 1) * 128],
                               W_g[k * 128:(k + 1) * 128, dc * 128:dc * 128 + 128])
                p.wait("scalar", "dma_sync", lv)
                cv = p.op("scalar", lambda e: e.copy(wgl[:, :], wtmp[:, :]))
                p.wait("sync", "ch_scalar", cv)
                bv = p.dma("sync", browf[:, :],
                           b_g.rearrange("a o -> o a")[:, dc * 128:dc * 128 + 128])
                p.wait("scalar", "dma_sync", bv)
                cb = p.op("scalar", lambda e: e.copy(brow[:, :], browf[:, :]))
                p.wait("tensor", "ch_scalar", cb)
                for tc2 in range(8):
                    b = tc2 % 2
                    for k in range(8):
                        p.op("tensor", lambda e, k=k, tc2=tc2, b=b: e.matmul(
                            pp[b][:, :], wgl[:, k * 128:(k + 1) * 128],
                            h_sb[:, k * T + tc2 * 512:k * T + tc2 * 512 + 512],
                            start=(k == 0), stop=False))
                    p.op("tensor", lambda e, b=b: e.matmul(
                        pp[b][:, :], brow[:, :], ones512[:, :],
                        start=False, stop=True), extra_sem="l1mm")
                    mv = p.val("l1mm")
                    p.wait("scalar", "l1mm", mv)
                    if gtv:
                        p.wait("scalar", "ch_vector", gtv)
                    ev = p.op("scalar", lambda e, b=b: e.activation(
                        gt[:, :], pp[b][:, :], AF.Sigmoid))
                    p.wait("tensor", "ch_scalar", ev)
                    p.wait("vector", "ch_scalar", ev)
                    if stv:
                        p.wait("vector", "pb_st", stv)
                    gtv = p.op("vector", lambda e, dc=dc, tc2=tc2: e.tensor_mul(
                        at_f[:, :], gt[:, :],
                        h_sb[:, dc * T + tc2 * 512:dc * T + tc2 * 512 + 512]))
                    av = p.op("vector", lambda e: e.tensor_copy(at_b[:, :],
                                                                at_f[:, :]))
                    p.wait("sync", "ch_vector", av)
                    stv = p.dma("sync",
                                attn1_fm[dc * 128:(dc + 1) * 128,
                                         tc2 * 512:tc2 * 512 + 512], at_b[:, :],
                                sem="pb_st")
            p.wait("sync", "dma_sync", p.val("dma_sync"))
            p.wait("sync", "pb_st", p.val("pb_st"))

        with p.scope():
            _phaseB()

    with p.scope():
        p.wait("tensor", "dma_sync", iv)
        emit_transpose(p, attn1_fm, attn1_tm, D, T, ident, "tat", src_dt=BF16)

    wr_sb = p.sb("wr_sb", [128, 4 * 512], BF16)
    wi_sb = p.sb("wi_sb", [128, 4 * 512], BF16)
    wineg_sb = p.sb("wineg_sb", [128, 4 * 512], BF16)
    bre_b = p.sb("bre_b", [1, 512], BF16)
    bim_b = p.sb("bim_b", [1, 512], BF16)
    def _phaseD():
        g1t = p.sb("g1t", [128, 32], F32)
        g2t = p.sb("g2t", [64, 16], F32)
        g1T = p.sb("g1T", [32, 128], F32)
        g2T = p.sb("g2T", [16, 64], F32)
        repa = p.sb("repa", [32, 512], F32)
        perio = p.sb("perio", [16, 128], F32)
        aee = p.sb("aee", [32, 512], F32)
        bee = p.sb("bee", [16, 512], F32)
        afull = p.sb("afull", [128, 512], F32)
        bfull = [p.sb("bfull%d" % c, [128, 512], F32) for c in range(4)]
        wacc = p.sb("wacc", [128, 512], F32)
        wtmp2 = p.sb("wtmp2", [128, 512], F32)
        brow = p.sb("brow", [1, 512], F32)
        pstf = p.ps("pstf", [128, 128], F32)
        pw = p.ps("pw", [128, 512], F32)
        bv1 = p.dma("sync", brow[:, :], b_re)
        p.wait("scalar", "dma_sync", bv1)
        c1 = p.op("scalar", lambda e: e.copy(bre_b[:, :], brow[:, :]))
        p.wait("sync", "ch_scalar", c1)
        bv2 = p.dma("sync", brow[:, :], b_im)
        p.wait("scalar", "dma_sync", bv2)
        p.op("scalar", lambda e: e.copy(bim_b[:, :], brow[:, :]))
        rv = p.dma("sync", repa[:, :], repac)
        rv = p.dma("sync", perio[:, :], perioc)
        p.fence()
        p.wait("tensor", "dma_sync", p.val("dma_sync"))
        for var, (g1ap, g2ap) in (("i", (G1I, G2I)), ("r", (G1R, G2R))):
            wdst = wr_sb if var == "r" else wi_sb
            lv1 = p.dma("sync", g1t[:, :], g1ap)
            lv2 = p.dma("sync", g2t[:, :], g2ap)
            p.fence()
            p.wait("tensor", "dma_sync", lv2)
            t1 = p.op("tensor", lambda e: e.transpose(
                pstf[0:32, :], g1t[:, :], identF[:, :]), extra_sem="l1mm")
            p.wait("vector", "l1mm", t1)
            e1 = p.op("vector", lambda e: e.tensor_copy(g1T[:, :], pstf[0:32, :]))
            p.wait("tensor", "ch_vector", e1)
            t2 = p.op("tensor", lambda e: e.transpose(
                pstf[0:16, 0:64], g2t[:, :], identF[0:64, 0:64]), extra_sem="l1mm")
            p.wait("vector", "l1mm", t2)
            e2 = p.op("vector", lambda e: e.tensor_copy(g2T[:, :], pstf[0:16, 0:64]))
            for c in range(4):
                bv = p.op("vector", lambda e, c=c: e.tensor_copy(
                    bee[:, :].rearrange("p (b e) -> p b e", e=16),
                    g2T[:, c * 16:(c + 1) * 16].rearrange(
                        "p (o e) -> p o e", o=1).broadcast_to([16, 32, 16])))
                p.wait("tensor", "ch_vector", bv)
                m1 = p.op("tensor", lambda e, c=c: e.matmul(
                    pw[:, :], perio[:, :], bee[:, :], start=True, stop=True),
                    extra_sem="l1mm")
                p.wait("vector", "l1mm", m1)
                ev = p.op("vector", lambda e, c=c: e.tensor_copy(
                    bfull[c][:, :], pw[:, :]))
                p.wait("tensor", "ch_vector", ev)
            for ck in range(4):
                for c in range(4):
                    av = p.op("vector", lambda e, c=c: e.tensor_copy(
                        aee[:, :].rearrange("p (b e) -> p b e", e=16),
                        g1T[:, c:128:4].rearrange(
                            "p (b o) -> p b o", o=1).broadcast_to([32, 32, 16])))
                    p.wait("tensor", "ch_vector", av)
                    m2 = p.op("tensor", lambda e, ck=ck: e.matmul(
                        pw[:, :], repa[:, ck * 128:(ck + 1) * 128], aee[:, :],
                        start=True, stop=True), extra_sem="l1mm")
                    p.wait("vector", "l1mm", m2)
                    ev2 = p.op("vector", lambda e: e.tensor_copy(afull[:, :],
                                                                 pw[:, :]))
                    p.wait("tensor", "ch_vector", ev2)
                    if c == 0:
                        p.op("vector", lambda e, c=c: e.tensor_mul(
                            wacc[:, :], afull[:, :], bfull[c][:, :]))
                    else:
                        p.op("vector", lambda e, c=c: e.tensor_mul(
                            wtmp2[:, :], afull[:, :], bfull[c][:, :]))
                        p.op("vector", lambda e: e.tensor_add(
                            wacc[:, :], wacc[:, :], wtmp2[:, :]))
                cv2 = p.op("vector", lambda e, ck=ck, wdst=wdst: e.tensor_copy(
                    wdst[:, ck * 512:(ck + 1) * 512], wacc[:, :]))
                if var == "i":
                    p.op("vector", lambda e, ck=ck: e.tensor_scalar_mul(
                        wacc[:, :], wacc[:, :], -1.0))
                    cv2 = p.op("vector", lambda e, ck=ck: e.tensor_copy(
                        wineg_sb[:, ck * 512:(ck + 1) * 512], wacc[:, :]))

    with p.scope():
        _phaseD()

    if DEBUG_L1:
        for ck in range(4):
            p.wait("sync", "ch_vector", p.chv("vector"))
            p.dma("sync", wr_dbg[ck * 128:(ck + 1) * 128, :], wr_sb[:, ck * 512:(ck + 1) * 512])
            p.dma("sync", wi_dbg[ck * 128:(ck + 1) * 128, :], wi_sb[:, ck * 512:(ck + 1) * 512])

    def _phaseE():
        x1t = p.sb("x1tE", [128, D], F32)
        a1t = p.sb("a1tE", [128, D], BF16)
        outt = p.sb("outtE", [128, D], F32)
        pr = p.ps("prE", [128, 512])
        pi = p.ps("piE", [128, 512])
        stv = 0
        evv = 0
        for n2 in range(32):
            if evv:
                p.wait("sync", "ch_vector", evv)
            lv1 = p.dma("sync", x1t[:, :], x1[n2:T:32, :])
            lv2 = p.dma("sync", a1t[:, :], attn1_tm[n2:T:32, :])
            if evv:
                p.wait("tensor", "ch_vector", evv)
            for ck in range(4):
                p.op("tensor", lambda e, ck=ck, n2=n2: e.matmul(
                    pr[:, :], xn1_sb[:, ck * T + n2:ck * T + n2 + 4065:32],
                    wr_sb[:, ck * 512:(ck + 1) * 512],
                    start=(ck == 0), stop=False))
            for ck in range(4):
                p.op("tensor", lambda e, ck=ck, n2=n2: e.matmul(
                    pr[:, :], xn1_sb[:, (4 + ck) * T + n2:(4 + ck) * T + n2 + 4065:32],
                    wineg_sb[:, ck * 512:(ck + 1) * 512],
                    start=False, stop=False))
            p.op("tensor", lambda e: e.matmul(
                pr[:, :], ones[:, :], bre_b[:, :], start=False, stop=True))
            for ck in range(4):
                p.op("tensor", lambda e, ck=ck, n2=n2: e.matmul(
                    pi[:, :], xn1_sb[:, ck * T + n2:ck * T + n2 + 4065:32],
                    wi_sb[:, ck * 512:(ck + 1) * 512],
                    start=(ck == 0), stop=False))
            for ck in range(4):
                p.op("tensor", lambda e, ck=ck, n2=n2: e.matmul(
                    pi[:, :], xn1_sb[:, (4 + ck) * T + n2:(4 + ck) * T + n2 + 4065:32],
                    wr_sb[:, ck * 512:(ck + 1) * 512],
                    start=False, stop=False))
            mvE = p.op("tensor", lambda e: e.matmul(
                pi[:, :], ones[:, :], bim_b[:, :], start=False, stop=True),
                extra_sem="l1mm")
            p.wait("vector", "l1mm", mvE)
            p.wait("vector", "dma_sync", lv2)
            if stv:
                p.wait("vector", "pe_st", stv)
            p.op("vector", lambda e: e.tensor_add(outt[:, :], x1t[:, :],
                                                  a1t[:, :]))
            p.op("vector", lambda e: e.tensor_add(outt[:, 0:512], outt[:, 0:512],
                                                  pr[:, :]))
            evv = p.op("vector", lambda e: e.tensor_add(outt[:, 512:1024],
                                                        outt[:, 512:1024], pi[:, :]))
            p.wait("sync", "ch_vector", evv)
            stv = p.dma("sync", OUT[n2:T:32, :], outt[:, :], sem="pe_st")
    with p.scope():
        _phaseE()
    p.wait("sync", "dma_sync", p.val("dma_sync"))
    p.wait("sync", "pe_st", p.val("pe_st"))


def build_l1():
    nc = mk_nc()
    xn1_fm = nc.dram_tensor("xn1_fm", [D, T], BF16, kind="ExternalInput").ap()
    x1 = nc.dram_tensor("x1", [T, D], F32, kind="ExternalInput").ap()
    Wz = {}
    bz = {}
    for nm in ("zf", "hf", "zs", "hs"):
        Wz[nm] = nc.dram_tensor("W_" + nm, [D, 512], F32, kind="ExternalInput").ap()
        bz[nm] = nc.dram_tensor("b_" + nm, [512, 1], F32, kind="ExternalInput").ap()
    W_g = nc.dram_tensor("W_g", [D, D], F32, kind="ExternalInput").ap()
    b_g = nc.dram_tensor("b_g", [D, 1], F32, kind="ExternalInput").ap()
    G1R = nc.dram_tensor("G1_R", [128, 32], F32, kind="ExternalInput").ap()
    G2R = nc.dram_tensor("G2_R", [64, 16], F32, kind="ExternalInput").ap()
    G1I = nc.dram_tensor("G1_I", [128, 32], F32, kind="ExternalInput").ap()
    G2I = nc.dram_tensor("G2_I", [64, 16], F32, kind="ExternalInput").ap()
    b_re = nc.dram_tensor("bias_real", [1, 512], F32, kind="ExternalInput").ap()
    b_im = nc.dram_tensor("bias_imag", [1, 512], F32, kind="ExternalInput").ap()
    onesc = nc.dram_tensor("onesc", [1, 128], BF16, kind="ExternalInput").ap()
    ones512c = nc.dram_tensor("ones512c", [1, 512], BF16, kind="ExternalInput").ap()
    repac = nc.dram_tensor("repac", [32, 512], F32, kind="ExternalInput").ap()
    perioc = nc.dram_tensor("perioc", [16, 128], F32, kind="ExternalInput").ap()
    identc = nc.dram_tensor("identc", [128, 128], BF16, kind="ExternalInput").ap()
    identf = nc.dram_tensor("identf", [128, 128], F32, kind="ExternalInput").ap()
    OUT = nc.dram_tensor("OUT", [T, D], F32, kind="ExternalOutput").ap()
    dbg = _l1_internal_drams(nc)
    with ExitStack() as st:
        p = Plan(nc, st)
        emit_l1(p, xn1_fm, x1, Wz, bz, W_g, b_g, G1R, G2R, G1I, G2I,
                b_re, b_im, onesc, ones512c, repac, perioc, identc, identf,
                OUT, dbg)
        p.run_block()
    return nc, ["OUT"]



def build_xp():
    nc = mk_nc()
    xn_fm = nc.dram_tensor("xn_fm", [D, T], BF16, kind="ExternalInput").ap()
    Wp = nc.dram_tensor("Wp", [D, 4 * D], F32, kind="ExternalInput").ap()
    bp = nc.dram_tensor("bp", [1, 4 * D], F32, kind="ExternalInput").ap()
    onesc = nc.dram_tensor("onesc", [1, 128], BF16, kind="ExternalInput").ap()
    xp_bs = nc.dram_tensor("xp_bs", [4, 128, 32 * D], BF16, kind="ExternalOutput").ap()

    with ExitStack() as st:
        p = Plan(nc, st)
        ones = p.sb("ones", [1, 128], BF16)
        p.dma("sync", ones[:, :], onesc)
        emit_xp(p, xn_fm, Wp, bp, ones, xp_bs)
        p.wait("sync", "dma_sync", p.val("dma_sync"))
        p.wait("sync", "xp_st", p.val("xp_st"))
        p.run_block()
    return nc, ["xp_bs"]


def emit_xp(p, xn_fm, Wp, bp, ones, xp_bs):
    with p.scope():
        xn_sb = p.sb("xn_sb", [128, 8 * T], BF16)
        for k in range(8):
            p.dma("sync", xn_sb[:, k * T:(k + 1) * T], xn_fm[k * 128:(k + 1) * 128, :])
        wtmp = p.sb("wtmp", [128, 8 * 512], F32)
        wp_sb = p.sb("wp_sb", [128, 8 * 512], BF16)
        brf = p.sb("brf", [1, 512], F32)
        brb = p.sb("brb", [1, 512], BF16)
        ot = p.sb("ot", [128, 512], BF16)
        pps = [p.ps("pxp%d" % b, [128, 512]) for b in range(2)]
        p.fence()
        p.wait("tensor", "dma_sync", p.val("dma_sync"))
        ev = {0: 0, 1: 0}
        stv = 0
        for s in range(4):
            for hh in range(2):
                for k in range(8):
                    lv = p.dma("sync", wtmp[:, k * 512:(k + 1) * 512],
                               Wp[k * 128:(k + 1) * 128,
                                  s * D + hh * 512:s * D + hh * 512 + 512])
                p.wait("scalar", "dma_sync", lv)
                cv = p.op("scalar", lambda e: e.copy(wp_sb[:, :], wtmp[:, :]))
                p.wait("sync", "ch_scalar", cv)
                bv = p.dma("sync", brf[:, :],
                           bp[:, s * D + hh * 512:s * D + hh * 512 + 512])
                p.wait("scalar", "dma_sync", bv)
                cb = p.op("scalar", lambda e: e.copy(brb[:, :], brf[:, :]))
                p.wait("tensor", "ch_scalar", cb)
                for n2 in range(32):
                    b = n2 % 2
                    if ev[b]:
                        p.wait("tensor", "ch_vector", ev[b])
                    for k in range(8):
                        p.op("tensor", lambda e, k=k, n2=n2, b=b: e.matmul(
                            pps[b][:, :],
                            xn_sb[:, k * T + n2:k * T + n2 + 4065:32],
                            wp_sb[:, k * 512:(k + 1) * 512],
                            start=(k == 0), stop=False))
                    p.op("tensor", lambda e, b=b: e.matmul(
                        pps[b][:, :], ones[:, :], brb[:, :],
                        start=False, stop=True), extra_sem="xpmm")
                    mv = p.val("xpmm")
                    p.wait("vector", "xpmm", mv)
                    if stv:
                        p.wait("vector", "xp_st", stv)
                    ev[b] = p.op("vector", lambda e, b=b: e.tensor_copy(
                        ot[:, :], pps[b][:, :]))
                    p.wait("sync", "ch_vector", ev[b])
                    stv = p.dma("sync",
                                xp_bs[s][:, n2 * D + hh * 512:n2 * D + hh * 512 + 512],
                                ot[:, :], sem="xp_st")
        p.wait("sync", "xp_st", p.val("xp_st"))


def build_ln1t():
    nc = mk_nc()
    X = nc.dram_tensor("X", [T, D], F32, kind="ExternalInput").ap()
    attn0 = nc.dram_tensor("attn0", [T, D], BF16, kind="ExternalInput").ap()
    ffn0 = nc.dram_tensor("ffn0", [T, D], F32, kind="ExternalInput").ap()
    identc = nc.dram_tensor("identc", [128, 128], BF16, kind="ExternalInput").ap()
    x1 = nc.dram_tensor("x1", [T, D], F32, kind="ExternalOutput").ap()
    xn1_fm = nc.dram_tensor("xn1_fm", [D, T], BF16, kind="ExternalOutput").ap()
    xn1_tmp = nc.dram_tensor("xn1_tmp", [T, D], F32).ap()
    xn1_tb = nc.dram_tensor("xn1_tb", [T, D], BF16).ap()
    with ExitStack() as st:
        p = Plan(nc, st)
        ident = p.sb("ident", [128, 128], BF16)
        iv = p.dma("sync", ident[:, :], identc)
        with p.scope():
            emit_ln(p, [X, attn0, ffn0], xn1_tmp, x1, None, "ln1",
                    xs_dts=[F32, BF16, F32], xn_bf16_ap=xn1_tb)
        p.wait("tensor", "dma_sync", iv)
        with p.scope():
            emit_transpose(p, xn1_tb, xn1_fm, T, D, ident, "t1", src_dt=BF16)
        p.run_block()
    return nc, ["x1", "xn1_fm"]


def emit_fwd_fft(p, tag, ncg, ybuf, ch_stride, Cre, Cim, Xre, Xim,
                 d1re, d1im, bd_ap, bd_bufs, parts=("mm1", "mm2")):
    if "mm1" not in parts:
        pass
    else:
        _emit_fwd_mm1(p, tag, ncg, ybuf, ch_stride, Cre, Cim, d1re, d1im)
    if "mm2" not in parts:
        return
    _emit_fwd_mm2(p, tag, ncg, Cre, Cim, Xre, Xim, bd_ap, bd_bufs)


def _emit_fwd_mm1(p, tag, ncg, ybuf, ch_stride, Cre, Cim, d1re, d1im):
    pmre = [p.ps(tag + "pm1r%d" % b, [128, 512]) for b in range(2)]
    pmim = [p.ps(tag + "pm1i%d" % b, [128, 512]) for b in range(2)]
    msv = {0: 0, 1: 0}
    mev = {0: 0, 1: 0}
    for cg in range(ncg):
        b = cg % 2
        if mev[b]:
            p.wait("tensor", "ch_vector", mev[b])
            p.wait("tensor", "ch_scalar", msv[b])
        for c in range(4):
            base = 4 * cg + c
            lhs = ybuf[:, base:base + 31 * ch_stride + 1:ch_stride]
            tp = (0, 32 * c)
            p.op("tensor", lambda e, lhs=lhs, b=b, c=c, tp=tp: e.matmul(
                pmre[b][32 * c:32 * c + 32, 0:129], lhs, d1re[:, :],
                start=True, stop=True, tile_position=tp))
            p.op("tensor", lambda e, lhs=lhs, b=b, c=c, tp=tp: e.matmul(
                pmim[b][32 * c:32 * c + 32, 0:129], lhs, d1im[:, :],
                start=True, stop=True, tile_position=tp),
                extra_sem=(tag + "m1") if c == 3 else None)
        mv = p.val(tag + "m1")
        p.wait("scalar", tag + "m1", mv)
        p.wait("vector", tag + "m1", mv)
        msv[b] = p.op("scalar", lambda e, cg=cg, b=b: e.copy(
            Cre[:, cg * 129:(cg + 1) * 129], pmre[b][:, 0:129]))
        mev[b] = p.op("vector", lambda e, cg=cg, b=b: e.tensor_copy(
            Cim[:, cg * 129:(cg + 1) * 129], pmim[b][:, 0:129]))


def _emit_fwd_mm2(p, tag, ncg, Cre, Cim, Xre, Xim, bd_ap, bd_bufs):
    p.wait("tensor", "ch_scalar", p.chv("scalar"))
    p.wait("tensor", "ch_vector", p.chv("vector"))

    pxre = [p.ps(tag + "pm2r%d" % b, [128, 256]) for b in range(2)]
    pxim = [p.ps(tag + "pm2i%d" % b, [128, 256]) for b in range(2)]
    xsv = {0: 0, 1: 0}
    xev = {0: 0, 1: 0}
    for k1g in range(0, K1, 4):
        b = (k1g // 4) % 2
        nk = min(4, K1 - k1g)
        if xev[b]:
            p.wait("sync", "ch_vector", xev[b])
            p.wait("tensor", "ch_vector", xev[b])
            p.wait("tensor", "ch_scalar", xsv[b])
        for j in range(nk):
            k1 = k1g + j
            p.dma("sync", bd_bufs[b][:, j * 384:(j + 1) * 384],
                  bd_ap[k1], sem=tag + "bd%d" % b)
        p.wait("tensor", tag + "bd%d" % b, p.val(tag + "bd%d" % b))
        for j in range(nk):
            k1 = k1g + j
            bo = j * 384
            p.op("tensor", lambda e, k1=k1, b=b, j=j, bo=bo: e.matmul(
                pxre[b][:, j * 64:j * 64 + ncg], bd_bufs[b][:, bo:bo + 128],
                Cre[:, k1::129], start=True, stop=False))
            p.op("tensor", lambda e, k1=k1, b=b, j=j, bo=bo: e.matmul(
                pxre[b][:, j * 64:j * 64 + ncg], bd_bufs[b][:, bo + 256:bo + 384],
                Cim[:, k1::129], start=False, stop=True))
            p.op("tensor", lambda e, k1=k1, b=b, j=j, bo=bo: e.matmul(
                pxim[b][:, j * 64:j * 64 + ncg],
                bd_bufs[b][:, bo + 128:bo + 256],
                Cre[:, k1::129], start=True, stop=False))
            p.op("tensor", lambda e, k1=k1, b=b, j=j, bo=bo: e.matmul(
                pxim[b][:, j * 64:j * 64 + ncg], bd_bufs[b][:, bo:bo + 128],
                Cim[:, k1::129], start=False, stop=True),
                extra_sem=(tag + "m2") if j == nk - 1 else None)
        mv = p.val(tag + "m2")
        p.wait("scalar", tag + "m2", mv)
        p.wait("vector", tag + "m2", mv)
        xsv[b] = p.op("scalar", lambda e, k1g=k1g, b=b, nk=nk: e.copy(
            Xre[:, k1g * ncg:(k1g + nk) * ncg].rearrange("p (j x) -> p j x", j=nk),
            pxre[b].ap().rearrange("p (j x) -> p j x", j=4)[:, 0:nk, 0:ncg]))
        xev[b] = p.op("vector", lambda e, k1g=k1g, b=b, nk=nk: e.tensor_copy(
            Xim[:, k1g * ncg:(k1g + nk) * ncg].rearrange("p (j x) -> p j x", j=nk),
            pxim[b].ap().rearrange("p (j x) -> p j x", j=4)[:, 0:nk, 0:ncg]))


def emit_filter(p, tag, quads, tmp1, tmp2):
    for (xr, xi, fr, fi, n) in quads:
        t1 = tmp1[:, 0:n].rearrange("p (k g) -> p k g", g=32) \
            if len(xr.shape) == 3 else tmp1[:, 0:n]
        t2 = tmp2[:, 0:n].rearrange("p (k g) -> p k g", g=32) \
            if len(xr.shape) == 3 else tmp2[:, 0:n]
        p.op("vector", lambda e, xr=xr, fr=fr, t1=t1: e.tensor_mul(t1, xr, fr))
        p.op("vector", lambda e, xr=xr, fi=fi, t2=t2: e.tensor_mul(t2, xr, fi))
        p.op("vector", lambda e, xr=xr, xi=xi, fi=fi: e.tensor_mul(xr, xi, fi))
        p.op("vector", lambda e, xr=xr, t1=t1: e.tensor_sub(xr, t1, xr))
        p.op("vector", lambda e, xi=xi, fr=fr: e.tensor_mul(xi, xi, fr))
        p.op("vector", lambda e, xi=xi, t2=t2: e.tensor_add(xi, xi, t2))


def emit_inv_fft(p, tag, ncg, Vre, Vim, iwre, iwim, iwnim,
                 SAre, SAim, SBre, SBim, ea_re, ea_nim, eb_re, eb_nim,
                 gbuf, outbuf):
    NC4 = 4 * ncg
    ps2a = [p.ps(tag + "pi2a%d" % b, [128, 256]) for b in range(2)]
    ps2b = [p.ps(tag + "pi2b%d" % b, [128, 256]) for b in range(2)]
    ssv = {0: 0, 1: 0}
    sev = {0: 0, 1: 0}
    for cg in range(ncg):
        b = cg % 2
        if sev[b]:
            p.wait("tensor", "ch_vector", sev[b])
            p.wait("tensor", "ch_scalar", ssv[b])
        for dst, k0, kn in ((ps2a, 0, K1A), (ps2b, K1A, K1B)):
            lhsr = Vre[:, cg::ncg][:, k0:k0 + kn]
            lhsi = Vim[:, cg::ncg][:, k0:k0 + kn]
            p.op("tensor", lambda e, lhsr=lhsr, b=b, dst=dst, kn=kn: e.matmul(
                dst[b][0:kn, 0:128], lhsr, iwre[:, :],
                start=True, stop=False))
            p.op("tensor", lambda e, lhsi=lhsi, b=b, dst=dst, kn=kn: e.matmul(
                dst[b][0:kn, 0:128], lhsi, iwnim[:, :],
                start=False, stop=True))
            p.op("tensor", lambda e, lhsr=lhsr, b=b, dst=dst, kn=kn: e.matmul(
                dst[b][0:kn, 128:256], lhsr, iwim[:, :],
                start=True, stop=False))
            p.op("tensor", lambda e, lhsi=lhsi, b=b, dst=dst, kn=kn: e.matmul(
                dst[b][0:kn, 128:256], lhsi, iwre[:, :],
                start=False, stop=True),
                extra_sem=(tag + "i2") if k0 == K1A else None)
        mv = p.val(tag + "i2")
        p.wait("scalar", tag + "i2", mv)
        p.wait("vector", tag + "i2", mv)
        dstA_re = SAre.rearrange("p (n c) -> p c n", c=NC4)[:, 4 * cg:4 * cg + 4, :]
        dstA_im = SAim.rearrange("p (n c) -> p c n", c=NC4)[:, 4 * cg:4 * cg + 4, :]
        dstB_re = SBre.ap().rearrange("p (n c) -> p c n", c=NC4)[:, 4 * cg:4 * cg + 4, :]
        dstB_im = SBim.ap().rearrange("p (n c) -> p c n", c=NC4)[:, 4 * cg:4 * cg + 4, :]
        p.op("scalar", lambda e, dstA_re=dstA_re, b=b: e.copy(
            dstA_re, ps2a[b][0:K1A, 0:128].rearrange("p (c n) -> p c n", c=4)))
        ssv[b] = p.op("scalar", lambda e, dstA_im=dstA_im, b=b: e.copy(
            dstA_im, ps2a[b][0:K1A, 128:256].rearrange("p (c n) -> p c n", c=4)))
        p.op("vector", lambda e, dstB_re=dstB_re, b=b: e.tensor_copy(
            dstB_re, ps2b[b][0:K1B, 0:128].rearrange("p (c n) -> p c n", c=4)))
        sev[b] = p.op("vector", lambda e, dstB_im=dstB_im, b=b: e.tensor_copy(
            dstB_im, ps2b[b][0:K1B, 128:256].rearrange("p (c n) -> p c n", c=4)))

    p.wait("tensor", "ch_scalar", p.chv("scalar"))
    p.wait("tensor", "ch_vector", p.chv("vector"))

    ps1 = [p.ps(tag + "pi1%d" % b, [128, 512]) for b in range(2)]
    oev = {0: 0, 1: 0}
    for n2 in range(32):
        b = n2 % 2
        if oev[b]:
            p.wait("tensor", "ch_vector", oev[b])
        ea_r = ea_re[:, n2:T:32]
        ea_n = ea_nim[:, n2:T:32]
        eb_r = eb_re[:, n2:T:32]
        eb_n = eb_nim[:, n2:T:32]
        p.op("tensor", lambda e, b=b, ea_r=ea_r, n2=n2: e.matmul(
            ps1[b][:, 0:NC4], ea_r, SAre[:, n2 * NC4:(n2 + 1) * NC4],
            start=True, stop=False))
        p.op("tensor", lambda e, b=b, ea_n=ea_n, n2=n2: e.matmul(
            ps1[b][:, 0:NC4], ea_n, SAim[:, n2 * NC4:(n2 + 1) * NC4],
            start=False, stop=False))
        p.op("tensor", lambda e, b=b, eb_r=eb_r, n2=n2: e.matmul(
            ps1[b][:, 0:NC4], eb_r, SBre[:, n2 * NC4:(n2 + 1) * NC4],
            start=False, stop=False))
        p.op("tensor", lambda e, b=b, eb_n=eb_n, n2=n2: e.matmul(
            ps1[b][:, 0:NC4], eb_n, SBim[:, n2 * NC4:(n2 + 1) * NC4],
            start=False, stop=True), extra_sem=tag + "i1")
        mv = p.val(tag + "i1")
        p.wait("vector", tag + "i1", mv)
        oev[b] = p.op("vector", lambda e, n2=n2, b=b: e.tensor_mul(
            outbuf[:, n2 * NC4:(n2 + 1) * NC4], ps1[b][:, 0:NC4],
            gbuf[:, n2 * NC4:(n2 + 1) * NC4]))


def _conv_quarter(p, tag, s, q, last, y_src_ap, g_src_ap, Fall, out_ap,
                  consts_sb, bd_ap):
    ncg = 64
    NC4 = 256
    d1re, d1im, iwre, iwim, iwnim, ea_re, ea_nim, eb_re, eb_nim = consts_sb
    ybuf = p.sb(tag + "y", [128, 32 * NC4], BF16)
    gbuf = p.sb(tag + "g", [128, 32 * NC4], BF16)
    outb = ybuf
    Cre = p.sb(tag + "cre", [128, 129 * ncg], BF16)
    Cim = p.sb(tag + "cim", [128, 129 * ncg], BF16)
    Xre = p.sb(tag + "xre", [128, 129 * ncg], BF16)
    Xim = p.sb(tag + "xim", [128, 129 * ncg], BF16)
    Fre = p.sb(tag + "fre", [128, 2 * 129 * 32], BF16)
    Fim = p.sb(tag + "fim", [128, 2 * 129 * 32], BF16)
    SBre = p.sb(tag + "sbre", [64, 32 * NC4], BF16)
    SBim = p.sb(tag + "sbim", [64, 32 * NC4], BF16)
    tmp1 = p.sb(tag + "t1", [128, 33 * 32], BF16)
    tmp2 = p.sb(tag + "t2", [128, 33 * 32], BF16)
    bd_bufs = [p.sb(tag + "bd%d" % b, [128, 4 * 384], BF16) for b in range(2)]
    SAre = Cre[0:65, 0:32 * NC4]
    SAim = Cim[0:65, 0:32 * NC4]

    for n2 in range(32):
        p.dma("sync", ybuf[:, n2 * NC4:(n2 + 1) * NC4],
              y_src_ap[:, n2 * D + q * NC4:n2 * D + q * NC4 + NC4])
        p.dma("sync", gbuf[:, n2 * NC4:(n2 + 1) * NC4],
              g_src_ap[:, n2 * D + q * NC4:n2 * D + q * NC4 + NC4])
    for rr in range(2):
        rank = 2 * q + rr
        p.dma("sync", Fre[:, rr * 4128:(rr + 1) * 4128], Fall[rank, s, 0])
        p.dma("sync", Fim[:, rr * 4128:(rr + 1) * 4128], Fall[rank, s, 1])
    p.fence()
    p.wait("tensor", "dma_sync", p.val("dma_sync"))
    p.wait("vector", "dma_sync", p.val("dma_sync"))
    p.wait("scalar", "dma_sync", p.val("dma_sync"))

    with p.scope():
        emit_fwd_fft(p, tag, ncg, ybuf.ap(), NC4, Cre, Cim, Xre, Xim, d1re, d1im,
                     bd_ap, bd_bufs)
    p.wait("vector", "ch_scalar", p.chv("scalar"))
    quads = []
    for rr in range(2):
        for kq in range(0, K1, 33):
            kn = min(33, K1 - kq)
            xr = Xre.ap().rearrange("p (k g) -> p k g", g=64)[
                :, kq:kq + kn, rr * 32:(rr + 1) * 32]
            xi = Xim.ap().rearrange("p (k g) -> p k g", g=64)[
                :, kq:kq + kn, rr * 32:(rr + 1) * 32]
            fr = Fre[:, rr * 4128:(rr + 1) * 4128].rearrange(
                "p (k g) -> p k g", g=32)[:, kq:kq + kn, :]
            fi = Fim[:, rr * 4128:(rr + 1) * 4128].rearrange(
                "p (k g) -> p k g", g=32)[:, kq:kq + kn, :]
            quads.append((xr, xi, fr, fi, kn * 32))
    emit_filter(p, tag, quads, tmp1, tmp2)
    p.wait("tensor", "ch_vector", p.chv("vector"))
    p.wait("scalar", "ch_vector", p.chv("vector"))
    emit_inv_fft(p, tag, ncg, Xre, Xim, iwre, iwim, iwnim,
                 SAre, SAim, SBre, SBim, ea_re, ea_nim, eb_re, eb_nim,
                 gbuf, outb)
    p.wait("sync", "ch_vector", p.chv("vector"))
    for n2 in range(32):
        if last:
            p.dma("sync", out_ap[n2:T:32, q * NC4:(q + 1) * NC4],
                  outb[:, n2 * NC4:(n2 + 1) * NC4], sem="cv_st")
        else:
            p.dma("sync", out_ap[:, n2 * D + q * NC4:n2 * D + q * NC4 + NC4],
                  outb[:, n2 * NC4:(n2 + 1) * NC4], sem="cv_st")
    p.wait("sync", "cv_st", p.val("cv_st"))


def _load_conv_consts(p, cl):
    d1re = p.sb("d1re", [128, 129], BF16)
    d1im = p.sb("d1im", [128, 129], BF16)
    iwre = p.sb("iwre", [128, 128], BF16)
    iwim = p.sb("iwim", [128, 128], BF16)
    iwnim = p.sb("iwnim", [128, 128], BF16)
    ea_re = p.sb("ea_re", [65, T], BF16)
    ea_nim = p.sb("ea_nim", [65, T], BF16)
    eb_re = p.sb("eb_re", [64, T], BF16)
    eb_nim = p.sb("eb_nim", [64, T], BF16)
    for nm, t in (("d1rec", d1re), ("d1imc", d1im), ("iwrec", iwre),
                  ("iwimc", iwim), ("iwnimc", iwnim), ("earec", ea_re),
                  ("eanimc", ea_nim), ("ebrec", eb_re), ("ebnimc", eb_nim)):
        p.dma("sync", t[:, :], cl[nm])
    return (d1re, d1im, iwre, iwim, iwnim, ea_re, ea_nim, eb_re, eb_nim)


def _declare_conv_consts(nc):
    cl = {}
    cl["d1rec"] = nc.dram_tensor("d1rec", [128, 129], BF16, kind="ExternalInput").ap()
    cl["d1imc"] = nc.dram_tensor("d1imc", [128, 129], BF16, kind="ExternalInput").ap()
    cl["iwrec"] = nc.dram_tensor("iwrec", [128, 128], BF16, kind="ExternalInput").ap()
    cl["iwimc"] = nc.dram_tensor("iwimc", [128, 128], BF16, kind="ExternalInput").ap()
    cl["iwnimc"] = nc.dram_tensor("iwnimc", [128, 128], BF16, kind="ExternalInput").ap()
    cl["earec"] = nc.dram_tensor("earec", [65, T], BF16, kind="ExternalInput").ap()
    cl["eanimc"] = nc.dram_tensor("eanimc", [65, T], BF16, kind="ExternalInput").ap()
    cl["ebrec"] = nc.dram_tensor("ebrec", [64, T], BF16, kind="ExternalInput").ap()
    cl["ebnimc"] = nc.dram_tensor("ebnimc", [64, T], BF16, kind="ExternalInput").ap()
    cl["bdc"] = nc.dram_tensor("bdc", [K1, 128, 384], BF16,
                               kind="ExternalInput").ap()
    return cl


def emit_conv_stage(p, s, tagpfx, y_in, xp_g, Fall, out, consts_sb, bdc_ap,
                    last):
    for q in range(4):
        with p.scope():
            _conv_quarter(p, tagpfx + "q%d" % q, s, q, last, y_in, xp_g, Fall,
                          out, consts_sb, bdc_ap)


def build_conv_stage(s):
    last = (s == 2)

    def build():
        nc = mk_nc()
        y_in = nc.dram_tensor("y_in", [128, 32 * D], BF16, kind="ExternalInput").ap()
        xp_g = nc.dram_tensor("xp_g", [128, 32 * D], BF16, kind="ExternalInput").ap()
        Fall = nc.dram_tensor("F_all", [8, 3, 2, 128, 32 * 129], BF16,
                              kind="ExternalInput").ap()
        cl = _declare_conv_consts(nc)
        if last:
            out = nc.dram_tensor("attn0", [T, D], BF16, kind="ExternalOutput").ap()
        else:
            out = nc.dram_tensor("y_out", [128, 32 * D], BF16,
                                 kind="ExternalOutput").ap()
        with ExitStack() as st:
            p = Plan(nc, st)
            consts_sb = _load_conv_consts(p, cl)
            emit_conv_stage(p, s, "c", y_in, xp_g, Fall, out, consts_sb,
                            cl["bdc"], last)
            p.wait("sync", "dma_sync", p.val("dma_sync"))
            p.run_block()
        return nc, ["attn0" if last else "y_out"]
    return build


def build_fprep():
    nc = mk_nc()
    pos_fm = nc.dram_tensor("pos_fm", [D, T], BF16, kind="ExternalInput").ap()
    Wf_sh = nc.dram_tensor("Wf_shard", [D, 3 * 128], F32, kind="ExternalInput").ap()
    bf_sh = nc.dram_tensor("bf_shard", [1, 3 * 128], F32, kind="ExternalInput").ap()
    onesc = nc.dram_tensor("onesc", [1, 128], BF16, kind="ExternalInput").ap()
    cl = _declare_conv_consts(nc)
    Fsh = nc.dram_tensor("F_shard", [3, 2, 128, 32 * 129], BF16,
                         kind="ExternalOutput").ap()
    with ExitStack() as st:
        p = Plan(nc, st)
        consts_sb = _load_conv_consts(p, cl)
        d1re, d1im = consts_sb[0], consts_sb[1]
        ones = p.sb("ones", [1, 128], BF16)
        p.dma("sync", ones[:, :], onesc)
        emit_fprep(p, pos_fm, Wf_sh, bf_sh, ones, d1re, d1im, cl["bdc"], Fsh)
        p.wait("sync", "dma_sync", p.val("dma_sync"))
        p.wait("sync", "fs_st", p.val("fs_st"))
        p.run_block()
    return nc, ["F_shard"]


def emit_fprep(p, pos_fm, Wf_sh, bf_sh, ones, d1re, d1im, bdc_ap, Fsh):
    with p.scope():
        pos_sb = p.sb("pos_sb", [128, 8 * T], BF16)
        for k in range(8):
            p.dma("sync", pos_sb[:, k * T:(k + 1) * T],
                  pos_fm[k * 128:(k + 1) * 128, :])
        ncg = 32
        for s in range(3):
            with p.scope():
                tag = "fp%d" % s
                wtmp = p.sb(tag + "wtmp", [128, 8 * 128], F32)
                wf_sb = p.sb(tag + "wf", [128, 8 * 128], BF16)
                brf = p.sb(tag + "brf", [1, 128], F32)
                brb = p.sb(tag + "brb", [1, 128], BF16)
                fbuf = p.sb(tag + "fb", [128, 32 * 128], BF16)
                Cre = p.sb(tag + "cre", [128, 129 * ncg], BF16)
                Cim = p.sb(tag + "cim", [128, 129 * ncg], BF16)
                Xre = p.sb(tag + "xre", [128, 129 * ncg], BF16)
                Xim = p.sb(tag + "xim", [128, 129 * ncg], BF16)
                bd_bufs = [p.sb(tag + "bd%d" % b, [128, 4 * 384], BF16)
                           for b in range(2)]
                for k in range(8):
                    lv = p.dma("sync", wtmp[:, k * 128:(k + 1) * 128],
                               Wf_sh[k * 128:(k + 1) * 128, s * 128:(s + 1) * 128])
                p.wait("scalar", "dma_sync", lv)
                cv = p.op("scalar", lambda e, wf_sb=wf_sb, wtmp=wtmp:
                          e.copy(wf_sb[:, :], wtmp[:, :]))
                p.wait("sync", "ch_scalar", cv)
                bv = p.dma("sync", brf[:, :], bf_sh[:, s * 128:(s + 1) * 128])
                p.wait("scalar", "dma_sync", bv)
                cb = p.op("scalar", lambda e, brb=brb, brf=brf:
                          e.copy(brb[:, :], brf[:, :]))
                p.wait("tensor", "ch_scalar", cb)
                pscope = p.scope(); pscope.__enter__()
                pf = [p.ps(tag + "pf%d" % b, [128, 128]) for b in range(2)]
                fev = {0: 0, 1: 0}
                for n2 in range(32):
                    b = n2 % 2
                    if fev[b]:
                        p.wait("tensor", "ch_vector", fev[b])
                    for k in range(8):
                        p.op("tensor", lambda e, k=k, n2=n2, b=b, pf=pf,
                             wf_sb=wf_sb: e.matmul(
                            pf[b][:, :], pos_sb[:, k * T + n2:k * T + n2 + 4065:32],
                            wf_sb[:, k * 128:(k + 1) * 128],
                            start=(k == 0), stop=False))
                    p.op("tensor", lambda e, b=b, pf=pf, brb=brb: e.matmul(
                        pf[b][:, :], ones[:, :], brb[:, :], start=False, stop=True),
                        extra_sem=tag + "fm")
                    mv = p.val(tag + "fm")
                    p.wait("vector", tag + "fm", mv)
                    fev[b] = p.op("vector", lambda e, n2=n2, b=b, pf=pf, fbuf=fbuf:
                                  e.tensor_copy(
                        fbuf[:, n2 * 128:(n2 + 1) * 128], pf[b][:, :]))
                p.wait("tensor", "ch_vector", p.chv("vector"))
                pscope.__exit__(None, None, None)
                with p.scope():
                    emit_fwd_fft(p, tag, ncg, fbuf.ap(), 128, Cre, Cim, Xre, Xim,
                                 d1re, d1im, bdc_ap, bd_bufs)
                p.wait("sync", "ch_vector", p.chv("vector"))
                p.wait("sync", "ch_scalar", p.chv("scalar"))
                p.dma("sync", Fsh[s, 0], Xre[:, :], sem="fs_st")
                p.dma("sync", Fsh[s, 1], Xim[:, :], sem="fs_st")
        p.wait("sync", "fs_st", p.val("fs_st"))


def build_s1():
    nc = mk_nc()
    X = nc.dram_tensor("X", [T, D], F32, kind="ExternalInput").ap()
    pos_fm = nc.dram_tensor("pos_fm", [D, T], BF16, kind="ExternalInput").ap()
    Wf_sh = nc.dram_tensor("Wf_shard", [D, 384], F32, kind="ExternalInput").ap()
    bf_sh = nc.dram_tensor("bf_shard", [1, 384], F32, kind="ExternalInput").ap()
    onesc = nc.dram_tensor("onesc", [1, 128], BF16, kind="ExternalInput").ap()
    d1rec = nc.dram_tensor("d1rec", [128, 129], BF16, kind="ExternalInput").ap()
    d1imc = nc.dram_tensor("d1imc", [128, 129], BF16, kind="ExternalInput").ap()
    bdc = nc.dram_tensor("bdc", [K1, 128, 384], BF16, kind="ExternalInput").ap()
    xn = nc.dram_tensor("xn_f32", [T, D], F32, kind="ExternalOutput").ap()
    xn_tm = nc.dram_tensor("xn_tm", [T, D], BF16, kind="ExternalOutput").ap()
    pmax = nc.dram_tensor("pmax", [128, 1], F32, kind="ExternalOutput").ap()
    Fsh = nc.dram_tensor("F_shard", [3, 2, 128, 32 * 129], BF16,
                         kind="ExternalOutput").ap()
    with ExitStack() as st:
        p = Plan(nc, st)
        d1re = p.sb("d1re", [128, 129], BF16)
        p.dma("sync", d1re[:, :], d1rec)
        d1im = p.sb("d1im", [128, 129], BF16)
        p.dma("sync", d1im[:, :], d1imc)
        ones = p.sb("ones", [1, 128], BF16)
        p.dma("sync", ones[:, :], onesc)
        with p.scope():
            emit_ln(p, [X], xn, None, pmax, "ln0", xn_bf16_ap=xn_tm)
        emit_fprep(p, pos_fm, Wf_sh, bf_sh, ones, d1re, d1im, bdc, Fsh)
        p.wait("sync", "dma_sync", p.val("dma_sync"))
        p.wait("sync", "fs_st", p.val("fs_st"))
        p.run_block()
    return nc, ["xn_f32", "xn_tm", "pmax", "F_shard"]


def build_s2():
    nc = mk_nc()
    xn = nc.dram_tensor("xn_f32", [T, D], F32, kind="ExternalInput").ap()
    xn_tm = nc.dram_tensor("xn_tm", [T, D], BF16, kind="ExternalInput").ap()
    pmax_all = nc.dram_tensor("pmax_all", [8, 128], F32, kind="ExternalInput").ap()
    clip_x = nc.dram_tensor("clip_x", [1, 1], F32, kind="ExternalInput").ap()
    w_gate = nc.dram_tensor("w_gate", [DI, D], F32, kind="ExternalInput").ap()
    w_up = nc.dram_tensor("w_up", [DI, D], F32, kind="ExternalInput").ap()
    Wp = nc.dram_tensor("Wp", [D, 4 * D], F32, kind="ExternalInput").ap()
    bp = nc.dram_tensor("bp", [1, 4 * D], F32, kind="ExternalInput").ap()
    identc = nc.dram_tensor("identc", [128, 128], BF16, kind="ExternalInput").ap()
    onesc = nc.dram_tensor("onesc", [1, 128], BF16, kind="ExternalInput").ap()
    h_out = nc.dram_tensor("h_f32", [DI, T], F32, kind="ExternalOutput").ap()
    pmax_h = nc.dram_tensor("pmax_h", [128, 1], F32, kind="ExternalOutput").ap()
    xp_bs = nc.dram_tensor("xp_bs", [4, 128, 32 * D], BF16,
                           kind="ExternalOutput").ap()
    scale_x = nc.dram_tensor("scale_x_d", [1, 1], F32).ap()
    xq_tm = nc.dram_tensor("xq_tm", [T, D], BF16).ap()
    xq_fm = nc.dram_tensor("xq_fm", [D, T], BF16).ap()
    xn_fm = nc.dram_tensor("xn_fm", [D, T], BF16).ap()
    with ExitStack() as st:
        p = Plan(nc, st)
        ident = p.sb("ident", [128, 128], BF16)
        iv = p.dma("sync", ident[:, :], identc)
        ones = p.sb("ones", [1, 128], BF16)
        p.dma("sync", ones[:, :], onesc)
        recip = p.sb("recip", [128, 1], F32)
        with p.scope():
            emit_scalar_combine(p, pmax_all, clip_x, scale_x, recip, "sc")
        with p.scope():
            emit_quant(p, xn, xq_tm, T, D, recip, p.val("dma_sync"), "q")
        with p.scope():
            emit_transpose(p, xq_tm, xq_fm, T, D, ident, "t1", src_dt=BF16)
        with p.scope():
            emit_transpose(p, xn_tm, xn_fm, T, D, ident, "t2", src_dt=BF16)
        emit_ffn0a(p, xq_fm, scale_x, w_gate, w_up, ident, h_out, pmax_h)
        p.full_barrier()
        emit_xp(p, xn_fm, Wp, bp, ones, xp_bs)
        p.wait("sync", "dma_sync", p.val("dma_sync"))
        p.wait("sync", "fa_st", p.val("fa_st"))
        p.wait("sync", "xp_st", p.val("xp_st"))
        p.run_block()
    return nc, ["h_f32", "pmax_h", "xp_bs"]


def build_s3():
    nc = mk_nc()
    h_f32 = nc.dram_tensor("h_f32", [DI, T], F32, kind="ExternalInput").ap()
    pmax_h_all = nc.dram_tensor("pmax_h_all", [8, 128], F32,
                                kind="ExternalInput").ap()
    clip_down = nc.dram_tensor("clip_down", [1, 1], F32, kind="ExternalInput").ap()
    w_down = nc.dram_tensor("w_down", [D, DI], F32, kind="ExternalInput").ap()
    xp_bs = nc.dram_tensor("xp_bs", [4, 128, 32 * D], BF16,
                           kind="ExternalInput").ap()
    Fall = nc.dram_tensor("F_all", [8, 3, 2, 128, 32 * 129], BF16,
                          kind="ExternalInput").ap()
    X = nc.dram_tensor("X", [T, D], F32, kind="ExternalInput").ap()
    Wz = {}
    bz = {}
    for nm in ("zf", "hf", "zs", "hs"):
        Wz[nm] = nc.dram_tensor("W_" + nm, [D, 512], F32, kind="ExternalInput").ap()
        bz[nm] = nc.dram_tensor("b_" + nm, [512, 1], F32, kind="ExternalInput").ap()
    W_g = nc.dram_tensor("W_g", [D, D], F32, kind="ExternalInput").ap()
    b_g = nc.dram_tensor("b_g", [D, 1], F32, kind="ExternalInput").ap()
    G1R = nc.dram_tensor("G1_R", [128, 32], F32, kind="ExternalInput").ap()
    G2R = nc.dram_tensor("G2_R", [64, 16], F32, kind="ExternalInput").ap()
    G1I = nc.dram_tensor("G1_I", [128, 32], F32, kind="ExternalInput").ap()
    G2I = nc.dram_tensor("G2_I", [64, 16], F32, kind="ExternalInput").ap()
    b_re = nc.dram_tensor("bias_real", [1, 512], F32, kind="ExternalInput").ap()
    b_im = nc.dram_tensor("bias_imag", [1, 512], F32, kind="ExternalInput").ap()
    onesc = nc.dram_tensor("onesc", [1, 128], BF16, kind="ExternalInput").ap()
    ones512c = nc.dram_tensor("ones512c", [1, 512], BF16, kind="ExternalInput").ap()
    repac = nc.dram_tensor("repac", [32, 512], F32, kind="ExternalInput").ap()
    perioc = nc.dram_tensor("perioc", [16, 128], F32, kind="ExternalInput").ap()
    identc = nc.dram_tensor("identc", [128, 128], BF16, kind="ExternalInput").ap()
    identf = nc.dram_tensor("identf", [128, 128], F32, kind="ExternalInput").ap()
    cl = _declare_conv_consts(nc)
    OUT = nc.dram_tensor("OUT", [T, D], F32, kind="ExternalOutput").ap()
    dbg = _l1_internal_drams(nc)
    ffn0 = nc.dram_tensor("ffn0", [T, D], F32).ap()
    y0 = nc.dram_tensor("y0", [128, 32 * D], BF16).ap()
    y1 = nc.dram_tensor("y1", [128, 32 * D], BF16).ap()
    attn0 = nc.dram_tensor("attn0", [T, D], BF16).ap()
    x1 = nc.dram_tensor("x1", [T, D], F32).ap()
    xn1_tmp = nc.dram_tensor("xn1_tmp", [T, D], F32).ap()
    xn1_tb = nc.dram_tensor("xn1_tb", [T, D], BF16).ap()
    xn1_fm = nc.dram_tensor("xn1_fm", [D, T], BF16).ap()
    with ExitStack() as st:
        p = Plan(nc, st)
        identS = p.sb("identS", [128, 128], BF16)
        p.dma("sync", identS[:, :], identc)
        with p.scope():
            emit_ffn0b(p, h_f32, pmax_h_all, clip_down, w_down, identS, ffn0)
        consts_sb = _load_conv_consts(p, cl)
        emit_conv_stage(p, 0, "c0", xp_bs[0], xp_bs[1], Fall, y0, consts_sb,
                        cl["bdc"], False)
        emit_conv_stage(p, 1, "c1", y0, xp_bs[2], Fall, y1, consts_sb,
                        cl["bdc"], False)
        emit_conv_stage(p, 2, "c2", y1, xp_bs[3], Fall, attn0, consts_sb,
                        cl["bdc"], True)
        with p.scope():
            emit_ln(p, [X, attn0, ffn0], xn1_tmp, x1, None, "ln1",
                    xs_dts=[F32, BF16, F32], xn_bf16_ap=xn1_tb)
        with p.scope():
            emit_transpose(p, xn1_tb, xn1_fm, T, D, identS, "tl1", src_dt=BF16)
        emit_l1(p, xn1_fm, x1, Wz, bz, W_g, b_g, G1R, G2R, G1I, G2I,
                b_re, b_im, onesc, ones512c, repac, perioc, identc, identf,
                OUT, dbg)
        p.wait("sync", "dma_sync", p.val("dma_sync"))
        p.wait("sync", "pe_st", p.val("pe_st"))
        p.run_block()
    return nc, ["OUT"]


def _spmd(build_fn, in_maps):
    nc, out_names = build_fn()
    res = run_bass_kernel_spmd(nc, in_maps, core_ids=list(range(len(in_maps))))
    return res.results


def kernel(**inputs):
    C = make_consts()
    X = np.asarray(inputs["L0_attn_X"], np.float32)
    Wp = np.ascontiguousarray(np.asarray(inputs["L0_attn_Wp"], np.float32))
    bp = np.asarray(inputs["L0_attn_bp"], np.float32).reshape(1, 4 * D)
    Wf = np.ascontiguousarray(np.asarray(inputs["L0_attn_Wf"], np.float32))
    bfv = np.asarray(inputs["L0_attn_bf"], np.float32)
    clip_x = np.asarray(inputs["L0_ffn_clip_x"], np.float32).reshape(1, 1)
    clip_down = np.asarray(inputs["L0_ffn_clip_down"], np.float32).reshape(1, 1)
    w_gate = np.ascontiguousarray(np.asarray(inputs["L0_ffn_w_gate"], np.float32))
    w_up = np.ascontiguousarray(np.asarray(inputs["L0_ffn_w_up"], np.float32))
    w_down = np.ascontiguousarray(np.asarray(inputs["L0_ffn_w_down"], np.float32))

    ident = np.asarray(C["ident"])
    identf = np.asarray(C["identf"])
    ones128 = np.asarray(C["ones128"])
    ones512 = np.asarray(C["ones512"])

    conv_consts = {
        "d1rec": np.asarray(C["D1re"]), "d1imc": np.asarray(C["D1im"]),
        "iwrec": np.asarray(C["iWre"]), "iwimc": np.asarray(C["iWim"]),
        "iwnimc": np.asarray(C["iWnegim"]),
        "earec": np.asarray(C["EreA"]), "eanimc": np.asarray(C["EnimA"]),
        "ebrec": np.asarray(C["EreB"]), "ebnimc": np.asarray(C["EnimB"]),
        "bdc": np.asarray(C["BDcat"])}

    maps = []
    for c in range(NCORES):
        cols = np.concatenate([np.arange(s * D + 128 * c, s * D + 128 * c + 128)
                               for s in range(3)])
        maps.append({"X": X[c], "pos_fm": np.asarray(C["pos_fm"]),
                     "Wf_shard": np.ascontiguousarray(Wf[:, cols]),
                     "bf_shard": np.ascontiguousarray(bfv[cols]).reshape(1, 384),
                     "onesc": ones128,
                     "d1rec": conv_consts["d1rec"], "d1imc": conv_consts["d1imc"],
                     "bdc": conv_consts["bdc"]})
    r = _spmd(build_s1, maps)
    xn_f32 = [r[c]["xn_f32"] for c in range(NCORES)]
    xn_tm = [r[c]["xn_tm"] for c in range(NCORES)]
    pmax_all = np.ascontiguousarray(
        np.stack([r[c]["pmax"][:, 0] for c in range(NCORES)]))
    F_all = np.ascontiguousarray(
        np.stack([r[c]["F_shard"] for c in range(NCORES)]))

    r = _spmd(build_s2, [{"xn_f32": xn_f32[c], "xn_tm": xn_tm[c],
                          "pmax_all": pmax_all, "clip_x": clip_x,
                          "w_gate": w_gate, "w_up": w_up, "Wp": Wp, "bp": bp,
                          "identc": ident, "onesc": ones128}
                         for c in range(NCORES)])
    h_f32 = [r[c]["h_f32"] for c in range(NCORES)]
    xp_bs = [r[c]["xp_bs"] for c in range(NCORES)]
    pmax_h_all = np.ascontiguousarray(
        np.stack([r[c]["pmax_h"][:, 0] for c in range(NCORES)]))

    l3_common = {
        "pmax_h_all": pmax_h_all, "clip_down": clip_down, "w_down": w_down,
        "F_all": F_all,
        "W_zf": np.ascontiguousarray(np.asarray(inputs["L1_attn_W_zf"], np.float32)),
        "b_zf": np.asarray(inputs["L1_attn_b_zf"], np.float32).reshape(512, 1),
        "W_hf": np.ascontiguousarray(np.asarray(inputs["L1_attn_W_hf"], np.float32)),
        "b_hf": np.asarray(inputs["L1_attn_b_hf"], np.float32).reshape(512, 1),
        "W_zs": np.ascontiguousarray(np.asarray(inputs["L1_attn_W_zs"], np.float32)),
        "b_zs": np.asarray(inputs["L1_attn_b_zs"], np.float32).reshape(512, 1),
        "W_hs": np.ascontiguousarray(np.asarray(inputs["L1_attn_W_hs"], np.float32)),
        "b_hs": np.asarray(inputs["L1_attn_b_hs"], np.float32).reshape(512, 1),
        "W_g": np.ascontiguousarray(np.asarray(inputs["L1_attn_W_g"], np.float32)),
        "b_g": np.asarray(inputs["L1_attn_b_g"], np.float32).reshape(D, 1),
        "G1_R": np.asarray(inputs["L1_ffn_G1_R"], np.float32).reshape(128, 32),
        "G2_R": np.asarray(inputs["L1_ffn_G2_R"], np.float32).reshape(64, 16),
        "G1_I": np.asarray(inputs["L1_ffn_G1_I"], np.float32).reshape(128, 32),
        "G2_I": np.asarray(inputs["L1_ffn_G2_I"], np.float32).reshape(64, 16),
        "bias_real": np.asarray(inputs["L1_ffn_bias_real"], np.float32).reshape(1, 512),
        "bias_imag": np.asarray(inputs["L1_ffn_bias_imag"], np.float32).reshape(1, 512),
        "onesc": ones128, "ones512c": ones512,
        "repac": np.asarray(C["REPA"]), "perioc": np.asarray(C["PERIO"]),
        "identc": ident, "identf": identf}
    l3_common.update(conv_consts)
    r = _spmd(build_s3, [dict(l3_common, h_f32=h_f32[c], xp_bs=xp_bs[c], X=X[c])
                         for c in range(NCORES)])
    out = np.stack([r[c]["OUT"] for c in range(NCORES)]).astype(np.float32)
    return out
